# version 17
# baseline (speedup 1.0000x reference)
import numpy as np
import ml_dtypes
import concourse.bass as bass
import concourse.mybir as mybir
from concourse.bass_utils import run_bass_kernel_spmd

F32 = mybir.dt.float32
BF16 = mybir.dt.bfloat16
I32 = mybir.dt.int32
Alu = mybir.AluOpType
Act = mybir.ActivationFunctionType
AX = mybir.AxisListType.X
bfdt = ml_dtypes.bfloat16

T, C, HP, DH, KVR = 4096, 1024, 8, 64, 128
R = 2048
NTILE = 16
NSLOT = 8
LN2C = 0.6931471805599453
EPS = 1e-5
OFFA = 10
OFFW = 14
NLAD = 16

DMA_ENGS = ("sync", "squeue", "vqueue", "gq")


class Sched:

    def __init__(self):
        self.ops = []
        self.ctr = {}

    def val(self, sem):
        return self.ctr.get(sem, 0)

    def em(self, eng, fn, waits=()):
        sem = "s_" + eng
        amt = 16 if eng in DMA_ENGS else 1
        self.ctr[sem] = self.ctr.get(sem, 0) + amt
        self.ops.append((eng, fn, list(waits), (sem, amt)))
        return self.ctr[sem]

    def emit(self, eng_names, eng, sems):
        for e, fn, waits, inc in self.ops:
            if e not in eng_names:
                continue
            for s, v in waits:
                if v > 0:
                    eng.wait_ge(sems[s], v)
            inst = fn(eng)
            assert inst is not None, f"op on {e} returned no instruction"
            inst.then_inc(sems[inc[0]], inc[1])


def _emit_round(S, em, V, S0, S1, S2, S3, I0, yq, scp):
    em("vector", lambda e: e.tensor_tensor(S1[:], S0[:], S0[:], Alu.mult))
    em("vector", lambda e: e.tensor_scalar(S2[:], S1[:], 1.0, -2.0, Alu.is_ge, Alu.mult))
    em("vector", lambda e: e.tensor_scalar(S3[:], S1[:], 4.0, -1.0, Alu.is_ge, Alu.mult))
    em("vector", lambda e: e.scalar_tensor_tensor(S1[:], S2[:], 4.0, S3[:], Alu.add, Alu.add))
    em("vector", lambda e: e.tensor_scalar(S3[:], S3[:], -0.5, 0.25, Alu.mult, Alu.add))
    em("vector", lambda e: e.scalar_tensor_tensor(S3[:], S2[:], -0.125, S3[:], Alu.mult, Alu.add))
    em("vector", lambda e: e.tensor_tensor(S2[:].rearrange("p (b c) -> p b c", c=32),
                                           S3[:].rearrange("p (b c) -> p b c", c=32),
                                           scp[:].unsqueeze(2).broadcast_to([128, 32, 32]), Alu.mult))
    em("vector", lambda e: e.tensor_tensor(S1[:], S0[:], S1[:], Alu.mult))
    em("vector", lambda e: e.tensor_scalar(S1[:], S1[:], 1.0, 0.5, Alu.mult, Alu.add))
    em("vector", lambda e: e.tensor_copy(I0[:], S1[:]))
    em("vector", lambda e: e.tensor_tensor(S0[:], I0[:], S1[:], Alu.is_gt))
    em("vector", lambda e: e.tensor_tensor(S0[:], I0[:], S0[:], Alu.subtract))
    em("vector", lambda e: e.tensor_tensor(yq[:], S0[:], S2[:], Alu.mult))


def build_program():
    import os
    KPHASE = int(os.environ.get("KPHASE", "9"))
    KNOBIAS = int(os.environ.get("KNOBIAS", "0"))
    KNTILE = int(os.environ.get("KNTILE", str(NTILE)))
    KSEC = int(os.environ.get("KSEC", "9"))
    KDBG = int(os.environ.get("KDBG", "0"))
    KATT = int(os.environ.get("KATT", "9"))
    nc = bass.Bass("TRN2", target_bir_lowering=False)

    x0 = nc.dram_tensor("x0", [R, C], F32, kind="ExternalInput").ap()
    fw = nc.dram_tensor("fw", [C, C], F32, kind="ExternalInput").ap()
    fb = nc.dram_tensor("fb", [1, C], F32, kind="ExternalInput").ap()
    fdw = nc.dram_tensor("fdw", [C, 32], F32, kind="ExternalInput").ap()
    fda = nc.dram_tensor("fda", [1, 1], F32, kind="ExternalInput").ap()
    wkd = nc.dram_tensor("wkd", [C, KVR], F32, kind="ExternalInput").ap()
    wku = nc.dram_tensor("wku", [KVR, 2048], F32, kind="ExternalInput").ap()
    wqa = nc.dram_tensor("wqa", [C, C], F32, kind="ExternalInput").ap()
    wo = nc.dram_tensor("wo", [C, C], F32, kind="ExternalInput").ap()
    thr = nc.dram_tensor("thr", [1, 2 * NLAD], F32, kind="ExternalInput").ap()
    maskh = nc.dram_tensor("maskh", [128, 4 * 256], BF16, kind="ExternalInput").ap()
    idt = nc.dram_tensor("idt", [128, 128], BF16, kind="ExternalInput").ap()
    out = nc.dram_tensor("out", [R, C], F32, kind="ExternalOutput").ap()
    dbgo = nc.dram_tensor("dbgo", [128, 4096], BF16, kind="ExternalOutput").ap()
    dbgf = nc.dram_tensor("dbgf", [128, 2048], F32, kind="ExternalOutput").ap()
    x1sp = nc.dram_tensor("x1sp", [R, C], F32)
    latmine = nc.dram_tensor("latmine", [128, R], BF16)
    latfull = nc.dram_tensor("latfull", [256, R], BF16)
    dab = nc.dram_tensor("dab", [1, 8], F32)
    drow = nc.dram_tensor("drow", [1, 512], F32)
    rrow = nc.dram_tensor("rrow", [1, 512], BF16)

    S = Sched()
    from contextlib import ExitStack
    es = ExitStack()
    sb = lambda name, shape, dt: es.enter_context(nc.sbuf_tensor(name, shape, dt))
    ps = lambda name, shape, dt: es.enter_context(nc.psum_tensor(name, shape, dt))

    xt = sb("xt", [128, 2, C], F32)
    x2t = sb("x2t", [128, 2, C], F32)
    outt = sb("outt", [128, 2, C], F32)
    qT = sb("qT", [128, HP, R], BF16)
    latT = sb("latT", [128, T], BF16)
    wqTf = sb("wqTf", [128, 8, C], BF16)
    wsh = sb("wsh", [128, 8, C], BF16)
    wkdb = sb("wkdb", [128, 8, KVR], BF16)
    wkub = sb("wkub", [128, 1024], BF16)
    wkuTb = sb("wkuTb", [128, HP, KVR], BF16)
    S0 = sb("S0", [128, C], F32)
    S1 = sb("S1", [128, C], F32)
    S2 = sb("S2", [128, C], BF16)
    S3 = sb("S3", [128, C], BF16)
    I0 = sb("I0", [128, C], I32)
    yq = sb("yq", [128, C], BF16)
    xn2 = sb("xn2", [128, C], BF16)
    xqT = sb("xqT", [128, 8, 128], BF16)
    xnT = sb("xnT", [128, 8, 128], BF16)
    vb = sb("vb", [128, 32, 2, 65], BF16)
    qaT = sb("qaT", [128, 2, R], BF16)
    yTall = sb("yTall", [128, HP, R], BF16)
    pb = sb("pb", [128, 2, 2, 2, 256], BF16)
    maskb = sb("maskb", [128, 4, 256], BF16)
    idtb = sb("idtb", [128, 128], BF16)
    onesb = sb("onesb", [1, 128], BF16)
    brad = sb("brad", [1, C], BF16)
    thrb = sb("thrb", [128, 2, NLAD], F32)
    thr2 = sb("thr2", [128, NLAD], F32)
    dasml = sb("dasml", [1, 8], F32)
    dabc = sb("dabc", [128, 8], F32)
    sm = sb("sm", [128, 16], F32)
    amx = sb("amx", [128, 32], F32)
    amn = sb("amn", [128, 32], F32)
    cnt = sb("cnt", [128, 32], F32)
    scp = sb("scp", [128, 32], BF16)
    invp = sb("invp", [128, 32], F32)
    eet = sb("eet", [128, 32], F32)
    gf = sb("gf", [128, 32], F32)
    dwt = sb("dwt", [128, 32], F32)
    ind = sb("ind", [128, 32, NLAD], BF16)
    Dcol = sb("Dcol", [128, 4], F32)
    rdcol = sb("rdcol", [128, 4], F32)
    rdcolb = sb("rdcolb", [128, 4], BF16)
    rdrow = sb("rdrow", [1, 512], BF16)
    ytmp = sb("ytmp", [128, 2, 256], F32)

    pSC = ps("pSC", [128, 2, 2, 2, 256], F32)
    pY = ps("pY", [128, 2, 2, 256], F32)
    pM = ps("pM", [128, 2, 512], F32)

    em, V = S.em, S.val
    pMv = pM.rearrange("p a b -> p (a b)")
    pMb = pM.bitcast(BF16)
    pSCv = pSC.rearrange("p a b c q -> p (a b c q)")

    def ln_quant(src, srcwaits, thr_idx, ksec=9):
        em("vector", lambda e: e.tensor_scalar(S0[:], src, 1.0 / C, 0.0, Alu.mult, Alu.add,
                                               accum_out=sm[:, 0:1]), srcwaits)
        em("vector", lambda e: e.scalar_tensor_tensor(S1[:], src, 1.0 / C, src, Alu.mult, Alu.mult,
                                                      accum_out=sm[:, 1:2]))
        em("vector", lambda e: e.scalar_tensor_tensor(sm[:, 3:4], sm[:, 0:1], -1.0, sm[:, 0:1],
                                                      Alu.mult, Alu.mult))
        em("vector", lambda e: e.tensor_scalar(S0[:], src, sm[:, 0:1], None, Alu.subtract))
        em("vector", lambda e: e.scalar_tensor_tensor(sm[:, 4:5], sm[:, 3:4], EPS, sm[:, 1:2],
                                                      Alu.add, Alu.add))
        DV0 = V("s_vector")
        em("scalar", lambda e: e.activation(sm[:, 5:6], sm[:, 4:5], Act.Ln), [("s_vector", DV0)])
        em("scalar", lambda e: e.activation(sm[:, 6:7], sm[:, 5:6], Act.Exp, scale=-0.5))
        em("scalar", lambda e: e.activation(sm[:, 7:8], sm[:, 5:6], Act.Exp, scale=0.5))
        ACT0 = V("s_scalar")
        if ksec == 0:
            return
        em("vector", lambda e: e.tensor_scalar(thr2[:], thrb[:, thr_idx, :], sm[:, 7:8], None,
                                               Alu.mult), [("s_scalar", ACT0)])
        em("vector", lambda e: e.reduce_max(amx[:], S0[:].rearrange("p (b c) -> p b c", c=32),
                                            axis=AX, apply_absolute_value=True))
        em("vector", lambda e: e.tensor_tensor(
            ind[:], amx[:].unsqueeze(2).broadcast_to([128, 32, NLAD]),
            thr2[:].unsqueeze(1).broadcast_to([128, 32, NLAD]), Alu.is_ge))
        em("vector", lambda e: e.reduce_sum(cnt[:], ind[:], axis=AX))
        em("vector", lambda e: e.drain())
        if ksec == 1:
            return
        em("vector", lambda e: e.tensor_scalar(amn[:], cnt[:], -LN2C, dabc[:, 2:3],
                                               Alu.mult, Alu.add))
        em("vector", lambda e: e.tensor_scalar(eet[:], cnt[:], LN2C, -(OFFA + 3) * LN2C,
                                               Alu.mult, Alu.add))
        DV1 = V("s_vector")
        em("scalar", lambda e: e.activation(invp[:], amn[:], Act.Exp), [("s_vector", DV1)])
        em("scalar", lambda e: e.activation(scp[:], eet[:], Act.Exp))
        ACT1 = V("s_scalar")
        em("vector", lambda e: e.tensor_scalar(gf[:], invp[:], sm[:, 6:7], 0.5, Alu.mult, Alu.mult),
           [("s_scalar", ACT1)])
        em("vector", lambda e: e.drain())
        em("vector", lambda e: e.tensor_tensor(S0[:].rearrange("p (b c) -> p b c", c=32),
                                               S0[:].rearrange("p (b c) -> p b c", c=32),
                                               gf[:].unsqueeze(2).broadcast_to([128, 32, 32]),
                                               Alu.mult))
        em("vector", lambda e: e.tensor_scalar(S0[:], S0[:], 3.0, -3.0, Alu.min, Alu.max))
        if ksec == 2:
            return
        _emit_round(S, em, V, S0, S1, S2, S3, I0, yq, scp)

    def transpose8(srcbuf, dst, wait_v):
        for g in range(2):
            for dt_ in range(4):
                em("tensor", lambda e, g=g, dt_=dt_: e.matmul(
                    pMb[:, g, 128 * dt_:128 * (dt_ + 1)],
                    srcbuf[:, 128 * (4 * g + dt_):128 * (4 * g + dt_ + 1)], idtb[:],
                    is_transpose=True, start=True, stop=True, skip_group_check=True),
                   [("s_vector", wait_v)] if (g == 0 and dt_ == 0) else [])
            em("vector", lambda e, g=g: e.tensor_copy(
                dst[:, 4 * g:4 * (g + 1), :].rearrange("p a b -> p (a b)"),
                pMb[:, g, 0:512]), [("s_tensor", V("s_tensor"))])

    def ld(dst, src, waits=()):
        return em("sync", lambda e, d=dst, s=src: e.dma_start(out=d, in_=s), waits)

    ld(thrb[:, 0, :], thr[0:1, 0:NLAD].broadcast_to([128, NLAD]))
    ld(thrb[:, 1, :], thr[0:1, NLAD:2 * NLAD].broadcast_to([128, NLAD]))
    ld(maskb[:].rearrange("p a b -> p (a b)"), maskh)
    ld(idtb[:], idt)
    ld(dasml[0:1, 0:1], fda)
    ld(xt[0:1, 0, :], fb)
    LD0 = V("s_sync")
    em("vector", lambda e: e.memset(onesb[:], 1.0), [("s_sync", LD0)])
    em("vector", lambda e: e.memset(vb[:, :, :, 64:65], 1.0))

    def dv(fn, waits=()):
        em("vector", fn, waits)
        em("vector", lambda e: e.drain())
    dv(lambda e: e.tensor_scalar(dasml[0:1, 1:2], dasml[0:1, 0:1], 8.0, None, Alu.mult))
    em("scalar", lambda e: e.activation(dasml[0:1, 2:3], dasml[0:1, 1:2], Act.Exp),
       [("s_vector", V("s_vector"))])
    dv(lambda e: e.tensor_scalar(dasml[0:1, 2:3], dasml[0:1, 2:3], 1.0, 1.0,
                                 Alu.mult, Alu.add), [("s_scalar", V("s_scalar"))])
    dv(lambda e: e.reciprocal(dasml[0:1, 3:4], dasml[0:1, 2:3]))
    dv(lambda e: e.tensor_scalar(dasml[0:1, 0:1], dasml[0:1, 3:4], -1.0, 0.5,
                                 Alu.mult, Alu.add))
    dv(lambda e: e.tensor_scalar(dasml[0:1, 4:5], dasml[0:1, 0:1], LN2C, None, Alu.mult))
    em("scalar", lambda e: e.activation(dasml[0:1, 1:2], dasml[0:1, 4:5], Act.Exp),
       [("s_vector", V("s_vector"))])
    dv(lambda e: e.tensor_scalar(dasml[0:1, 5:6], dasml[0:1, 4:5], -1.0, None, Alu.mult))
    em("scalar", lambda e: e.activation(dasml[0:1, 3:4], dasml[0:1, 5:6], Act.Exp),
       [("s_vector", V("s_vector"))])
    dv(lambda e: e.tensor_scalar(dasml[0:1, 2:3], dasml[0:1, 4:5], -1.0,
                                 (OFFA + 3) * LN2C, Alu.mult, Alu.add))
    DA_ACT = V("s_scalar")
    em("sync", lambda e: e.dma_start(out=dab.ap(), in_=dasml[0:1, :]),
       [("s_scalar", DA_ACT), ("s_vector", V("s_vector"))])
    em("sync", lambda e: e.dma_start(out=dabc[:], in_=dab.ap().broadcast_to([128, 8])),
       [("s_sync", V("s_sync"))])
    DA_BC = V("s_sync")
    em("vector", lambda e: e.tensor_scalar(brad[0:1, :], xt[0:1, 0, :], dabc[0:1, 3:4], 0.5,
                                           Alu.mult, Alu.mult), [("s_sync", DA_BC)])

    w0 = V("s_vector")
    ld(xt[:, 1, :], wku[:, 0:1024], [("s_vector", w0)])
    em("vector", lambda e: e.tensor_copy(wsh[:, 0, :], xt[:, 1, :]), [("s_sync", V("s_sync"))])
    WKU_DONE = V("s_vector")
    for hp in range(HP):
        em("tensor", lambda e, hp=hp: e.matmul(
            pMb[:, hp % 2, 0:128], wsh[:, 0, 128 * hp:128 * (hp + 1)], idtb[:],
            is_transpose=True, start=True, stop=True, skip_group_check=True),
           [("s_vector", WKU_DONE)] if hp == 0 else
           ([("s_vector", V("s_vector"))] if hp >= 2 else []))
        em("vector", lambda e, hp=hp: e.tensor_copy(wkuTb[:, hp, :], pMb[:, hp % 2, 0:128]),
           [("s_tensor", V("s_tensor"))])
    WT_DONE = V("s_vector")
    WT_T = V("s_tensor")
    w0 = V("s_vector")
    ld(xt[:, 1, :], wku[:, 1024:2048], [("s_vector", w0)])
    em("vector", lambda e: e.tensor_copy(wkub[:, 0:1024], xt[:, 1, :]), [("s_sync", V("s_sync"))])
    for k in range(8):
        w0 = V("s_vector")
        em("squeue", lambda e, k=k: e.dma_start(out=xt[:, 0, :], in_=wqa[128 * k:128 * (k + 1), :]),
           [("s_vector", w0)])
        em("vector", lambda e, k=k: e.tensor_scalar(wsh[:, k, :], xt[:, 0, :], 0.125, None, Alu.mult),
           [("s_squeue", V("s_squeue"))] + ([("s_tensor", WT_T)] if k == 0 else []))
    for k in range(8):
        w0 = V("s_vector")
        ld(xt[:, 1, 0:KVR], wkd[128 * k:128 * (k + 1), :], [("s_vector", w0)])
        em("vector", lambda e, k=k: e.tensor_copy(wkdb[:, k, :], xt[:, 1, 0:KVR]),
           [("s_sync", V("s_sync"))])

    fw_read = {}
    for ot in range(8):
        stg = ot % 2
        if ot < 2:
            wv = V("s_vector")
        else:
            wv = fw_read[ot - 2]
        ld(xt[:, stg, :], fw[128 * ot:128 * (ot + 1), :], [("s_vector", wv)])
        WL = V("s_sync")
        ld(dwt[:], fdw[128 * ot:128 * (ot + 1), :], [("s_vector", V("s_vector"))])
        WL2 = V("s_sync")
        em("vector", lambda e: e.tensor_scalar(dwt[:], dwt[:], 8.0, None, Alu.mult), [("s_sync", WL2)])
        em("vector", lambda e: e.drain())
        em("scalar", lambda e: e.activation(dwt[:], dwt[:], Act.Exp), [("s_vector", V("s_vector"))])
        em("vector", lambda e: e.tensor_scalar(dwt[:], dwt[:], 1.0, 1.0, Alu.mult, Alu.add),
           [("s_scalar", V("s_scalar"))])
        em("vector", lambda e: e.drain())
        em("vector", lambda e: e.reciprocal(dwt[:], dwt[:]))
        em("vector", lambda e: e.drain())
        em("vector", lambda e, stg=stg: e.reduce_max(amx[:], xt[:, stg, :].rearrange("p (b c) -> p b c", c=32),
                                                     axis=AX, apply_absolute_value=True),
           [("s_sync", WL)])
        em("vector", lambda e: e.tensor_tensor(
            ind[:], amx[:].unsqueeze(2).broadcast_to([128, 32, NLAD]),
            thrb[:, 1, :].unsqueeze(1).broadcast_to([128, 32, NLAD]), Alu.is_ge))
        em("vector", lambda e: e.reduce_sum(cnt[:], ind[:], axis=AX))
        em("vector", lambda e: e.drain())
        em("vector", lambda e: e.tensor_scalar(eet[:], cnt[:], -LN2C, (OFFW + 2.5) * LN2C,
                                               Alu.mult, Alu.add))
        em("vector", lambda e: e.drain())
        em("vector", lambda e: e.scalar_tensor_tensor(eet[:], dwt[:], LN2C, eet[:], Alu.mult, Alu.add))
        DVW = V("s_vector")
        em("scalar", lambda e: e.activation(invp[:], eet[:], Act.Exp), [("s_vector", DVW)])
        em("vector", lambda e: e.tensor_scalar(amn[:], cnt[:], LN2C, -(OFFW + 1.5) * LN2C,
                                               Alu.mult, Alu.add))
        em("vector", lambda e: e.drain())
        em("vector", lambda e: e.scalar_tensor_tensor(amn[:], dwt[:], -LN2C, amn[:], Alu.mult, Alu.add))
        DVW2 = V("s_vector")
        em("scalar", lambda e: e.activation(scp[:], amn[:], Act.Exp), [("s_vector", DVW2)])
        ACTW = V("s_scalar")
        em("vector", lambda e: e.tensor_scalar(gf[:], invp[:], 0.5, None, Alu.mult),
           [("s_scalar", ACTW)])
        em("vector", lambda e: e.drain())
        em("vector", lambda e, stg=stg: e.tensor_tensor(S0[:].rearrange("p (b c) -> p b c", c=32),
                                                        xt[:, stg, :].rearrange("p (b c) -> p b c", c=32),
                                                        gf[:].unsqueeze(2).broadcast_to([128, 32, 32]),
                                                        Alu.mult))
        fw_read[ot] = V("s_vector")
        em("vector", lambda e: e.tensor_scalar(S0[:], S0[:], 3.0, -3.0, Alu.min, Alu.max))
        _emit_round(S, em, V, S0, S1, S2, S3, I0, yq, scp)
        YQD = V("s_vector")
        for g in range(2):
            for dt_ in range(4):
                em("tensor", lambda e, g=g, dt_=dt_: e.matmul(
                    pMb[:, g, 128 * dt_:128 * (dt_ + 1)],
                    yq[:, 128 * (4 * g + dt_):128 * (4 * g + dt_ + 1)], idtb[:],
                    is_transpose=True, start=True, stop=True, skip_group_check=True),
                   [("s_vector", YQD)] if (g == 0 and dt_ == 0) else [])
            em("vector", lambda e, g=g, ot=ot: e.tensor_copy(
                wqTf[:, 4 * g:4 * (g + 1), 128 * ot:128 * (ot + 1)],
                pMb[:, g, 0:512].rearrange("p (a b) -> p a b", a=4)),
               [("s_tensor", V("s_tensor"))])
    P0_V = V("s_vector")
    P0_G = V("s_gpsimd")
    if KPHASE < 1:
        return _finish(S, nc, es, V)

    tile_v = {}
    tile_g = {}
    x1sp_dma = {}
    for t in range(KNTILE):
        p = t % 2
        waits = []
        if t < 2:
            waits += [("s_vector", P0_V), ("s_gpsimd", P0_G)]
        else:
            waits += [("s_vector", tile_v[t - 2]), ("s_gpsimd", tile_g[t - 2])]
        ld(xt[:, p, :], x0[128 * t:128 * (t + 1), :], waits)
        XL = V("s_sync")
        if KSEC < 0:
            tile_v[t] = V("s_vector")
            continue
        ln_quant(xt[:, p, :], [("s_sync", XL)], 0, KSEC)
        tile_g[t] = V("s_gpsimd")
        YQD = V("s_vector")
        if KDBG == 1 and t == 0:
            em("vector", lambda e: e.tensor_copy(S1[:], yq[:]))
            em("vector", lambda e: e.tensor_copy(amn[:], scp[:]))
            em("sync", lambda e: e.dma_start(out=out[0:128, :], in_=S1[:]),
               [("s_vector", V("s_vector"))])
            em("sync", lambda e: e.dma_start(out=out[128:256, 0:16], in_=sm[:]))
            em("sync", lambda e: e.dma_start(out=out[128:256, 32:64], in_=amx[:]))
            em("sync", lambda e: e.dma_start(out=out[128:256, 64:96], in_=cnt[:]))
            em("sync", lambda e: e.dma_start(out=out[128:256, 96:128], in_=invp[:]))
            em("sync", lambda e: e.dma_start(out=out[128:256, 128:160], in_=amn[:]))
            em("sync", lambda e: e.dma_start(out=out[256:384, :], in_=S0[:]))
            DBGD = V("s_sync")
            em("vector", lambda e: e.tensor_copy(S1[:], S2[:]), [("s_sync", DBGD)])
            em("sync", lambda e: e.dma_start(out=out[384:512, :], in_=S1[:]),
               [("s_vector", V("s_vector"))])
            em("vector", lambda e: e.engine_nop(), [("s_sync", V("s_sync"))])
        if KSEC < 1:
            tile_v[t] = V("s_vector")
            continue
        transpose8(yq, xqT, YQD)
        TQD = V("s_vector")
        if KSEC < 2:
            tile_v[t] = V("s_vector")
            continue
        for nh in range(2):
            for k in range(8):
                em("tensor", lambda e, nh=nh, k=k: e.matmul(
                    pSCv[:, 512 * nh:512 * (nh + 1)], xqT[:, k, :], wqTf[:, k, 512 * nh:512 * (nh + 1)],
                    start=(k == 0), stop=(k == 7), skip_group_check=True),
                   [("s_vector", TQD)] if (nh == 0 and k == 0) else [])
            if not KNOBIAS:
                em("tensor", lambda e, nh=nh: e.matmul(
                    pSCv[:, 512 * nh:512 * (nh + 1)], onesb[0:1, :], brad[0:1, 512 * nh:512 * (nh + 1)],
                    start=False, stop=True, skip_group_check=True))
        MMF = V("s_tensor")
        em("vector", lambda e, p=p: e.scalar_tensor_tensor(
            x2t[:, p, :], pSCv[:, 0:1024], dabc[:, 1:2], xt[:, p, :], Alu.mult, Alu.add),
           [("s_tensor", MMF)] + ([("s_gq", x1sp_dma[t - 2])] if t >= 2 else []))
        tile_v[t] = V("s_vector")
        em("gq", lambda e, t=t, p=p: e.dma_start(out=x1sp.ap()[128 * t:128 * (t + 1), :],
                                                 in_=x2t[:, p, :]),
           [("s_vector", tile_v[t])])
        x1sp_dma[t] = V("s_gq")
        x1t = x2t[:, p, :]
        if KSEC < 3:
            continue
        em("vector", lambda e, x1t=x1t: e.tensor_scalar(S0[:], x1t, 1.0 / C, 0.0, Alu.mult, Alu.add,
                                                        accum_out=sm[:, 0:1]))
        em("vector", lambda e, x1t=x1t: e.scalar_tensor_tensor(S1[:], x1t, 1.0 / C, x1t,
                                                               Alu.mult, Alu.mult, accum_out=sm[:, 1:2]))
        em("vector", lambda e: e.scalar_tensor_tensor(sm[:, 3:4], sm[:, 0:1], -1.0, sm[:, 0:1],
                                                      Alu.mult, Alu.mult))
        em("vector", lambda e: e.drain())
        em("vector", lambda e: e.scalar_tensor_tensor(sm[:, 4:5], sm[:, 3:4], EPS, sm[:, 1:2],
                                                      Alu.add, Alu.add))
        DLN = V("s_vector")
        em("scalar", lambda e: e.activation(sm[:, 5:6], sm[:, 4:5], Act.Ln), [("s_vector", DLN)])
        em("scalar", lambda e: e.activation(sm[:, 6:7], sm[:, 5:6], Act.Exp, scale=-0.5))
        em("vector", lambda e, x1t=x1t: e.tensor_scalar(xn2[:], x1t, sm[:, 0:1], sm[:, 6:7],
                                                        Alu.subtract, Alu.mult),
           [("s_scalar", V("s_scalar"))])
        XN2D = V("s_vector")
        if KSEC < 4:
            continue
        transpose8(xn2, xnT, XN2D)
        TND = V("s_vector")
        if KSEC < 5:
            continue
        for k in range(8):
            em("tensor", lambda e, k=k: e.matmul(
                pM[:, 0, 0:128], wkdb[:, k, :], xnT[:, k, :], start=(k == 0), stop=(k == 7)),
               [("s_vector", TND)] if k == 0 else [])
        em("vector", lambda e, t=t: e.tensor_copy(latT[:, 128 * t:128 * (t + 1)], pM[:, 0, 0:128]),
           [("s_tensor", V("s_tensor"))])
        em("squeue", lambda e, t=t: e.dma_start(out=latmine.ap()[:, 128 * t:128 * (t + 1)],
                                                in_=latT[:, 128 * t:128 * (t + 1)]),
           [("s_vector", V("s_vector"))])
        if KDBG == 5 and t == 0:
            em("vector", lambda e: e.tensor_copy(S1[:], xn2[:]))
            em("sync", lambda e: e.dma_start(out=out[0:128, :], in_=S1[:]),
               [("s_vector", V("s_vector"))])
            em("vector", lambda e: e.tensor_copy(S1[:, 0:128], latT[:, 0:128]),
               [("s_sync", V("s_sync"))])
            em("sync", lambda e: e.dma_start(out=out[128:256, 0:128], in_=S1[:, 0:128]),
               [("s_vector", V("s_vector"))])
            em("vector", lambda e: e.engine_nop(), [("s_sync", V("s_sync"))])
        if KSEC < 6:
            continue
        for hp in range(HP):
            for k in range(8):
                em("tensor", lambda e, hp=hp, k=k: e.matmul(
                    pM[:, 1, 0:128], wsh[:, k, 128 * hp:128 * (hp + 1)], xnT[:, k, :],
                    start=(k == 0), stop=(k == 7)),
                   [("s_vector", V("s_vector"))] if k == 0 else [])
            em("vector", lambda e, hp=hp, t=t: e.tensor_copy(qT[:, hp, 128 * t:128 * (t + 1)],
                                                             pM[:, 1, 0:128]),
               [("s_tensor", V("s_tensor"))])
    P1_SYNC = V("s_sync")
    P1_SQ = V("s_squeue")
    P1_GQ = V("s_gq")
    P1_V = V("s_vector")
    if KPHASE < 2:
        return _finish(S, nc, es, V)

    em("gcc", lambda e: e.collective_compute(
        "AllGather", Alu.bypass, replica_groups=[[0, 1], [2, 3], [4, 5], [6, 7]],
        ins=[latmine.ap().opt()], outs=[latfull.ap().opt()]),
       [("s_sync", P1_SYNC), ("s_squeue", P1_SQ)])
    CC = V("s_gcc")
    for hh in range(2):
        for j in range(NSLOT):
            em("squeue", lambda e, hh=hh, j=j: e.dma_start(
                out=latT[:, 512 * j + 256 * hh:512 * j + 256 * hh + 256],
                in_=latfull.ap()[128 * hh:128 * (hh + 1), 256 * j:256 * (j + 1)]),
               [("s_gcc", CC)] if (hh == 0 and j == 0) else [])
    LATR = V("s_squeue")
    if KDBG == 6:
        for hh in range(2):
            em("vector", lambda e, hh=hh: e.tensor_copy(S1[:], latT[:, 0:1024]) if False else
               e.engine_nop(), [])
        S.ops.pop(); S.ctr["s_vector"] -= 1
        S.ops.pop(); S.ctr["s_vector"] -= 1
        for hh in range(2):
            for cc2 in range(2):
                em("squeue", lambda e, hh=hh, cc2=cc2: e.dma_start(
                    out=yq[:], in_=latfull.ap()[128 * hh:128 * (hh + 1), 1024 * cc2:1024 * (cc2 + 1)]),
                   [("s_gcc", CC)] + ([("s_vector", V("s_vector"))] if (hh + cc2) else []))
                em("vector", lambda e: e.tensor_copy(S1[:], yq[:]), [("s_squeue", V("s_squeue"))])
                em("sync", lambda e, hh=hh, cc2=cc2: e.dma_start(
                    out=out[128 * (2 * hh + cc2):128 * (2 * hh + cc2 + 1), :], in_=S1[:]),
                   [("s_vector", V("s_vector"))])
        em("vector", lambda e: e.engine_nop(), [("s_sync", V("s_sync"))])
    if KDBG == 10:
        em("vector", lambda e: e.tensor_copy(S1[:, 0:520],
                                             vb[:, 0:4, :, :].rearrange("p a b c -> p (a b c)")),
           [("s_squeue", LATR)])
        em("sync", lambda e: e.dma_start(out=out[0:128, 0:520], in_=S1[:, 0:520]),
           [("s_vector", V("s_vector"))])
        em("vector", lambda e: e.tensor_copy(S1[:], qaT[:, 0, 0:1024]), [("s_sync", V("s_sync"))])
        em("sync", lambda e: e.dma_start(out=out[128:256, :], in_=S1[:]),
           [("s_vector", V("s_vector"))])
        em("vector", lambda e: e.engine_nop(), [("s_sync", V("s_sync"))])
    if KDBG == 8:
        em("vector", lambda e: e.tensor_copy(S1[:], qT[:, 0, 0:1024]), [("s_squeue", LATR)])
        em("sync", lambda e: e.dma_start(out=out[0:128, :], in_=S1[:]),
           [("s_vector", V("s_vector"))])
        em("vector", lambda e: e.tensor_copy(S1[:, 0:128], wkuTb[:, 0, :]), [("s_sync", V("s_sync"))])
        em("vector", lambda e: e.tensor_copy(S1[:, 128:256], wkub[:, 0:128]))
        em("vector", lambda e: e.tensor_copy(S1[:, 256:512], latT[:, 0:256]))
        em("sync", lambda e: e.dma_start(out=out[128:256, 0:512], in_=S1[:, 0:512]),
           [("s_vector", V("s_vector"))])
        em("vector", lambda e: e.engine_nop(), [("s_sync", V("s_sync"))])
    if KDBG == 3:
        for c4 in range(4):
            em("vector", lambda e, c4=c4: e.tensor_copy(S1[:], latT[:, 1024 * c4:1024 * (c4 + 1)]),
               [("s_squeue", LATR)] + ([("s_sync", V("s_sync"))] if c4 > 0 else []))
            em("sync", lambda e, c4=c4: e.dma_start(out=out[128 * c4:128 * (c4 + 1), :], in_=S1[:]),
               [("s_vector", V("s_vector"))])
        em("vector", lambda e: e.engine_nop(), [("s_sync", V("s_sync"))])
    if KPHASE < 3:
        return _finish(S, nc, es, V)

    epi_v = {0: 0, 1: 0}
    epi_gq = {0: 0, 1: 0}
    epi_ytmp = {0: 0, 1: 0}
    for hp in range(HP):
        wv = V("s_vector")
        wow = [("s_vector", max(wv, P1_V))]
        if hp == 0 and tile_g:
            wow.append(("s_gpsimd", tile_g[max(tile_g)]))
        ld(xt[:, 1, :], wo[128 * hp:128 * (hp + 1), :], wow)
        WO_LD = V("s_sync")
        for h01 in range(2 if KATT >= 1 else 0):
            qw = [("s_squeue", LATR), ("s_scalar", V("s_scalar")), ("s_vector", V("s_vector"))] \
                if (hp == 0 and h01 == 0) else [("s_scalar", V("s_scalar")),
                                                ("s_vector", V("s_vector"))]
            for cch in range(4):
                em("tensor", lambda e, hp=hp, h01=h01, cch=cch: e.matmul(
                    pSCv[:, 512 * cch:512 * (cch + 1)],
                    wkuTb[64 * h01:64 * (h01 + 1), hp, :],
                    qT[64 * h01:64 * (h01 + 1), hp, 512 * cch:512 * (cch + 1)],
                    start=True, stop=True, skip_group_check=True),
                   qw if cch == 0 else [])
            MMQ = V("s_tensor")
            em("vector", lambda e, h01=h01: e.tensor_copy(qaT[:, h01, :], pSCv[:, 0:2048]),
               [("s_tensor", MMQ)])
        QAD = V("s_vector")
        for g in range(8 if KATT >= 2 else 0):
            vw = [("s_vector", V("s_vector"))]
            for kt4 in range(4):
                kt = 4 * g + kt4
                for h01 in range(2):
                    em("tensor", lambda e, kt=kt, kt4=kt4, h01=h01, hp=hp: e.matmul(
                        pMv[:, 64 * (2 * kt4 + h01):64 * (2 * kt4 + h01) + 64],
                        latT[:, 128 * kt:128 * (kt + 1)],
                        wkub[:, 128 * hp + 64 * h01:128 * hp + 64 * (h01 + 1)],
                        start=True, stop=True, skip_group_check=True),
                       vw if (kt4 == 0 and h01 == 0) else [])
            MMV = V("s_tensor")
            if KATT >= 3:
                em("vector", lambda e, g=g: e.tensor_copy(
                    vb[:, 4 * g:4 * (g + 1), :, 0:64],
                    pMv[:, 0:512].rearrange("p (a b c) -> p a b c", a=4, b=2)),
                   [("s_tensor", MMV)])
        VD = V("s_vector")
        if KDBG == 12 and hp == 0:
            em("sync", lambda e: e.dma_start(
                out=dbgo[:, 0:1040], in_=vb[:, 0:8, :, :].rearrange("p a b c -> p (a b c)")),
               [("s_vector", VD), ("s_tensor", V("s_tensor"))])
            em("sync", lambda e: e.dma_start(out=dbgo[:, 1024 + 0:1024 + 2048], in_=qaT[:, 0, :]))
            em("vector", lambda e: e.engine_nop(), [("s_sync", V("s_sync"))])
        if KDBG == 11 and hp == 0:
            em("vector", lambda e: e.tensor_copy(S1[:, 0:520],
                                                 vb[:, 0:4, :, :].rearrange("p a b c -> p (a b c)")),
               [("s_vector", VD)])
            em("sync", lambda e: e.dma_start(out=out[0:128, 0:520], in_=S1[:, 0:520]),
               [("s_vector", V("s_vector"))])
            em("vector", lambda e: e.tensor_copy(S1[:], qaT[:, 0, 0:1024]), [("s_sync", V("s_sync"))])
            em("sync", lambda e: e.dma_start(out=out[128:256, :], in_=S1[:]),
               [("s_vector", V("s_vector"))])
            em("vector", lambda e: e.engine_nop(), [("s_sync", V("s_sync"))])
        for j in range(NSLOT if KATT >= 9 else 0):
            nkt = 4 * j + 4
            nb = nkt // 2
            sp = j % 2
            pending_av = None
            for kb in range(nb):
                buf = kb % 2
                scw = [("s_vector", VD), ("s_scalar", max(0, V("s_scalar") - 1))]
                if kb == 0:
                    scw += [("s_vector", epi_ytmp[sp])]
                for kk in range(2):
                    kt = 2 * kb + kk
                    for h01 in range(2):
                        em("tensor", lambda e, kt=kt, kk=kk, h01=h01, buf=buf, j=j: e.matmul(
                            pSC[:, buf, h01, kk, :], latT[:, 128 * kt:128 * (kt + 1)],
                            qaT[:, h01, 256 * j:256 * (j + 1)], start=True, stop=True,
                            skip_group_check=True),
                           scw if (kk == 0 and h01 == 0) else [])
                MMS = V("s_tensor")
                em("scalar", lambda e, buf=buf: e.activation(
                    pb.rearrange("p a b c q -> p a (b c q)")[:, buf, :],
                    pSCv[:, 1024 * buf:1024 * (buf + 1)], Act.Exp), [("s_tensor", MMS)])
                EXPD = V("s_scalar")

                def make_av(kb=kb, buf=buf, sp=sp, nkt=nkt, nb=nb, EXPD=EXPD):
                    avw = [("s_scalar", EXPD)]
                    if kb >= nb - 2:
                        r = kb - (nb - 2)
                        em("vector", lambda e, buf=buf, r=r: e.tensor_tensor(
                            pb[:, buf, :, :, :], pb[:, buf, :, :, :],
                            maskb[:, 2 * r:2 * r + 2, :].unsqueeze(1).broadcast_to([128, 2, 2, 256]),
                            Alu.mult), [("s_scalar", EXPD)])
                        avw = [("s_vector", V("s_vector"))]
                    for kk in range(2):
                        kt = 2 * kb + kk
                        for h01 in range(2):
                            em("tensor", lambda e, kt=kt, h01=h01, buf=buf, sp=sp, kk=kk, nkt=nkt: e.matmul(
                                pY[0:65, h01, sp, :], vb[:, kt, h01, :], pb[:, buf, h01, kk, :],
                                start=(kt == 0), stop=(kt == nkt - 1), skip_group_check=True),
                               avw if (kk == 0 and h01 == 0) else [])

                if pending_av is not None:
                    pending_av()
                pending_av = make_av
            pending_av()
            AVD = V("s_tensor")
            em("vector", lambda e, sp=sp: e.tensor_copy(ytmp[0:65, :, :], pY[0:65, :, sp, :]),
               [("s_tensor", AVD), ("s_gq", V("s_gq"))])
            YE = V("s_vector")
            epi_ytmp[sp] = YE
            if KDBG == 16 and hp == 0 and j == 0:
                em("sync", lambda e: e.dma_start(out=dbgf[:, 0:512],
                                                 in_=ytmp[:].rearrange("p a b -> p (a b)")),
                   [("s_vector", YE)])
                DSY = V("s_sync")
                em("vector", lambda e: e.engine_nop(), [("s_sync", DSY)])
                em("tensor", lambda e: e.nop(), [("s_sync", DSY)])
                em("scalar", lambda e: e.nop(), [("s_sync", DSY)])
            em("gq", lambda e: e.dma_start(out=drow.ap(), in_=ytmp[64:65, :, :]),
               [("s_vector", YE)])
            em("gq", lambda e: e.dma_start(
                out=Dcol[:], in_=drow.ap().rearrange("a (p c) -> (a p) c", p=128)),
               [("s_gq", V("s_gq"))])
            em("vector", lambda e: e.reciprocal(rdcol[:], Dcol[:]), [("s_gq", V("s_gq"))])
            em("vector", lambda e: e.drain())
            em("vector", lambda e: e.tensor_copy(rdcolb[:], rdcol[:]))
            em("gq", lambda e: e.dma_start(
                out=rrow.ap().rearrange("a (p c) -> (a p) c", p=128), in_=rdcolb[:]),
               [("s_vector", V("s_vector"))])
            em("gq", lambda e: e.dma_start(out=rdrow[:], in_=rrow.ap()), [("s_gq", V("s_gq"))])
            RD = V("s_gq")
            em("tensor", lambda e: e.matmul(pMv[:, 0:512], onesb[0:1, :], rdrow[0:1, :],
                                            start=True, stop=True, skip_group_check=True),
               [("s_gq", RD), ("s_vector", V("s_vector"))])
            BCD = V("s_tensor")
            em("gq", lambda e: e.dma_start(out=ytmp[64:128, 1, :], in_=ytmp[0:64, 1, :]),
               [("s_vector", YE)])
            YMV = V("s_gq")
            em("vector", lambda e, hp=hp, j=j: e.tensor_tensor(
                yTall[0:64, hp, 256 * j:256 * (j + 1)],
                ytmp[0:64, 0, :], pMv[0:64, 0:256], Alu.mult),
               [("s_tensor", BCD), ("s_vector", YE)])
            em("vector", lambda e, hp=hp, j=j: e.tensor_tensor(
                yTall[64:128, hp, 256 * j:256 * (j + 1)],
                ytmp[64:128, 1, :], pMv[64:128, 256:512], Alu.mult),
               [("s_gq", YMV)])
            epi_v[sp] = V("s_vector")
            epi_gq[sp] = V("s_gq")
            if KDBG == 13 and hp == 0 and j == 0:
                em("sync", lambda e: e.dma_start(out=dbgf[:, 0:512],
                                                 in_=ytmp[:].rearrange("p a b -> p (a b)")),
                   [("s_vector", V("s_vector")), ("s_gq", V("s_gq"))])
                em("sync", lambda e: e.dma_start(out=dbgf[:, 512:516], in_=rdcol[:]))
                em("sync", lambda e: e.dma_start(out=dbgf[:, 516:520], in_=Dcol[:]))
                em("sync", lambda e: e.dma_start(
                    out=dbgo[:, 0:512],
                    in_=pb[:, 0, :, :, :].rearrange("p a b q -> p (a b q)")[:, 0:512]))
                em("sync", lambda e: e.dma_start(out=dbgo[0:1, 512:1024], in_=rdrow[0:1, :]))
                em("vector", lambda e: e.engine_nop(), [("s_sync", V("s_sync"))])
            if KDBG == 9 and hp == 0 and j == 0:
                em("vector", lambda e: e.tensor_copy(S1[:], qaT[:, 0, 0:1024]))
                em("sync", lambda e: e.dma_start(out=out[256:384, :], in_=S1[:]),
                   [("s_vector", V("s_vector"))])
                em("vector", lambda e: e.tensor_copy(S1[:, 0:520],
                                                     vb[:, 0:4, :, :].rearrange("p a b c -> p (a b c)")),
                   [("s_sync", V("s_sync"))])
                em("sync", lambda e: e.dma_start(out=out[384:512, 0:520], in_=S1[:, 0:520]),
                   [("s_vector", V("s_vector"))])
                em("sync", lambda e: e.dma_start(out=out[0:128, 0:512],
                                                 in_=ytmp[:].rearrange("p a b -> p (a b)")))
                em("vector", lambda e: e.engine_nop(), [("s_sync", V("s_sync"))])
            if KDBG == 7 and hp == 0 and j == 0:
                EPIV = V("s_vector")
                em("sync", lambda e: e.dma_start(out=out[0:128, 0:512],
                                                 in_=ytmp[:].rearrange("p a b -> p (a b)")),
                   [("s_vector", EPIV)])
                em("sync", lambda e: e.dma_start(out=out[128:256, 0:4], in_=rdcol[:]))
                em("gq", lambda e: e.dma_start(out=out[256:384, 0:512],
                                               in_=qaT[:, :, 0:256]
                                               .rearrange("p a b -> p a b")),
                   [("s_vector", EPIV)])
                em("gq", lambda e: e.dma_start(out=out[384:512, 0:260],
                                               in_=vb[:, 0:2, :, :].rearrange("p a b c -> p (a b c)")),
                   [("s_vector", EPIV)])
                em("vector", lambda e: e.engine_nop(),
                   [("s_sync", V("s_sync")), ("s_gq", V("s_gq"))])
        em("vector", lambda e, hp=hp: e.tensor_copy(wsh[:, hp, :], xt[:, 1, :]),
           [("s_sync", WO_LD)])
    if KDBG == 17:
        em("sync", lambda e: e.dma_start(out=dbgo[:, 0:2048], in_=yTall[:, 0, :]),
           [("s_vector", V("s_vector"))])
        em("sync", lambda e: e.dma_start(out=dbgo[:, 2048:4096], in_=yTall[:, 7, :]))
        DSY = V("s_sync")
        em("vector", lambda e: e.engine_nop(), [("s_sync", DSY)])
        em("tensor", lambda e: e.nop(), [("s_sync", DSY)])
    if KDBG == 14:
        for hpx in range(HP):
            em("sync", lambda e, hpx=hpx: e.dma_start(out=dbgo[:, 256 * hpx:256 * (hpx + 1)],
                                                      in_=yTall[:, hpx, 0:256]),
               [("s_vector", V("s_vector"))] if hpx == 0 else [])
        em("vector", lambda e: e.engine_nop(), [("s_sync", V("s_sync"))])
    P3_V = V("s_vector")
    WO_DONE = P3_V
    if KPHASE < 4:
        return _finish(S, nc, es, V)

    if KDBG == 18:
        em("sync", lambda e: e.dma_start(out=dbgo[:, 0:1024], in_=wsh[:, 0, :]),
           [("s_vector", P3_V)])
        em("sync", lambda e: e.dma_start(out=dbgo[:, 1024:2048], in_=wsh[:, 7, :]))
        em("sync", lambda e: e.dma_start(out=dbgo[:, 2048:3072], in_=wsh[:, 1, :]))
        DSY = V("s_sync")
        em("vector", lambda e: e.engine_nop(), [("s_sync", DSY)])
        em("tensor", lambda e: e.nop(), [("s_sync", DSY)])
    x1ld = {}
    tile_v4 = {}
    for t in range(NTILE):
        p = t % 2
        xw = [("s_vector", P3_V)] if t < 2 else [("s_vector", tile_v4[t - 2])]
        if t == 0:
            xw.append(("s_gq", P1_GQ))
        em("sync", lambda e, t=t, p=p: e.dma_start(out=xt[:, p, :],
                                                   in_=x1sp.ap()[128 * t:128 * (t + 1), :]), xw)
        x1ld[t] = V("s_sync")
        zw = [("s_vector", P3_V if t == 0 else V("s_vector")), ("s_vector", WO_DONE)]
        for nh in range(2):
            for hp in range(HP):
                em("tensor", lambda e, t=t, nh=nh, hp=hp: e.matmul(
                    pM[:, nh, :], yTall[:, hp, 128 * t:128 * (t + 1)],
                    wsh[:, hp, 512 * nh:512 * (nh + 1)], start=(hp == 0), stop=(hp == 7)),
                   zw if (nh == 0 and hp == 0) else [])
        MMZ = V("s_tensor")
        em("vector", lambda e, t=t, p=p: e.scalar_tensor_tensor(
            x2t[:, p, :], pMv[:, 0:1024], 1.0, xt[:, p, :], Alu.mult, Alu.add),
           [("s_tensor", MMZ), ("s_sync", x1ld[t])])
        X2D = V("s_vector")
        if KDBG == 15 and t == 0:
            em("sync", lambda e: e.dma_start(out=dbgf[:, 0:1024], in_=x2t[:, 0, :]),
               [("s_vector", X2D)])
            em("vector", lambda e: e.engine_nop(), [("s_sync", V("s_sync"))])
        ln_quant(x2t[:, p, :], [("s_vector", X2D)], 0)
        YQD = V("s_vector")
        transpose8(yq, xqT, YQD)
        TQD = V("s_vector")
        for nh in range(2):
            for k in range(8):
                em("tensor", lambda e, nh=nh, k=k: e.matmul(
                    pSCv[:, 512 * nh:512 * (nh + 1)], xqT[:, k, :], wqTf[:, k, 512 * nh:512 * (nh + 1)],
                    start=(k == 0), stop=(k == 7), skip_group_check=True),
                   [("s_vector", TQD)] if (nh == 0 and k == 0) else [])
            em("tensor", lambda e, nh=nh: e.matmul(
                pSCv[:, 512 * nh:512 * (nh + 1)], onesb[0:1, :], brad[0:1, 512 * nh:512 * (nh + 1)],
                start=False, stop=True, skip_group_check=True))
        MMF = V("s_tensor")
        ow = [("s_tensor", MMF)]
        if t >= 2:
            ow.append(("s_vqueue", 16 * (t - 1)))
        em("vector", lambda e, p=p: e.scalar_tensor_tensor(
            outt[:, p, :], pSCv[:, 0:1024], dabc[:, 1:2], x2t[:, p, :], Alu.mult, Alu.add), ow)
        tile_v4[t] = V("s_vector")
        em("vqueue", lambda e, t=t, p=p: e.dma_start(out=out[128 * t:128 * (t + 1), :],
                                                     in_=outt[:, p, :]),
           [("s_vector", V("s_vector"))])
    return _finish(S, nc, es, V)


def _finish(S, nc, es, V):
    final_waits = [(s, V(s)) for s in ["s_vqueue", "s_vector", "s_scalar", "s_tensor",
                                       "s_gpsimd", "s_gq", "s_squeue"] if V(s) > 0]
    sem_names = ["s_sync", "s_squeue", "s_vqueue", "s_gq", "s_gcc", "s_tensor", "s_vector",
                 "s_scalar", "s_gpsimd"]
    with nc.Block() as block:
        sems = {s: es.enter_context(nc.semaphore(s)) for s in sem_names}

        @block.sync
        def _(sync):
            S.emit(("sync",), sync, sems)
            for s, v in final_waits:
                sync.wait_ge(sems[s], v)

        @block.vector
        def _(v):
            S.emit(("vector",), v, sems)

        @block.scalar
        def _(s):
            S.emit(("scalar", "squeue", "vqueue"), s, sems)

        @block.tensor
        def _(tn):
            S.emit(("tensor",), tn, sems)

        @block.gpsimd
        def _(g):
            S.emit(("gpsimd", "gq", "gcc"), g, sems)

    es.close()
    return nc


_CACHE = {}


def _get_nc():
    if "nc" not in _CACHE:
        _CACHE["nc"] = build_program()
    return _CACHE["nc"]


def _host_consts(h):
    key = f"c{h}"
    if key in _CACHE:
        return _CACHE[key]
    m = np.zeros((128, 4, 256), dtype=bfdt)
    kp = np.arange(128)[:, None]
    qf = np.arange(256)[None, :]
    for r in range(4):
        off = 128 * r - 256 * h
        m[:, r, :] = ((off + kp) <= qf).astype(bfdt)
    mask = np.ascontiguousarray(m.reshape(128, 4 * 256))
    thrv = np.zeros((1, 2 * NLAD), dtype=np.float32)
    thrv[0, :NLAD] = 2.0 ** np.arange(-OFFA, -OFFA + NLAD)
    thrv[0, NLAD:] = 2.0 ** np.arange(-OFFW, -OFFW + NLAD)
    idv = np.eye(128, dtype=bfdt)
    _CACHE[key] = (mask, thrv, idv)
    return _CACHE[key]


def kernel(L0_attn_x, L0_attn_w_kv_down, L0_attn_w_kv_up, L0_attn_w_q, L0_attn_w_out,
           L0_ffn_weight, L0_ffn_bias, L0_ffn_delta_w, L0_ffn_delta_act):
    x = np.ascontiguousarray(np.asarray(L0_attn_x, dtype=np.float32))
    wkdv = np.ascontiguousarray(np.asarray(L0_attn_w_kv_down, dtype=np.float32))
    wkuv = np.ascontiguousarray(np.asarray(L0_attn_w_kv_up, dtype=np.float32))
    wqv = np.ascontiguousarray(np.asarray(L0_attn_w_q, dtype=np.float32))
    wov = np.ascontiguousarray(np.asarray(L0_attn_w_out, dtype=np.float32))
    fwv = np.ascontiguousarray(np.asarray(L0_ffn_weight, dtype=np.float32))
    fbv = np.ascontiguousarray(np.asarray(L0_ffn_bias, dtype=np.float32)).reshape(1, C)
    fdwv = np.ascontiguousarray(np.asarray(L0_ffn_delta_w, dtype=np.float32))
    fdav = np.ascontiguousarray(np.asarray(L0_ffn_delta_act, dtype=np.float32)).reshape(1, 1)

    nc = _get_nc()
    in_maps = []
    rowsets = []
    for d in range(8):
        b, h = d // 2, d % 2
        mask, thrv, idv = _host_consts(h)
        rows = np.concatenate([np.arange(256 * (2 * j + h), 256 * (2 * j + h) + 256)
                               for j in range(NSLOT)])
        rowsets.append((b, rows))
        in_maps.append(dict(
            x0=np.ascontiguousarray(x[b][rows]),
            fw=fwv, fb=fbv, fdw=fdwv, fda=fdav,
            wkd=wkdv, wku=wkuv, wqa=wqv, wo=wov,
            thr=thrv, maskh=mask, idt=idv,
        ))
    res = run_bass_kernel_spmd(nc, in_maps, core_ids=list(range(8)))
    _CACHE["res"] = res
    outp = np.empty((4, T, C), dtype=np.float32)
    for d in range(8):
        b, rows = rowsets[d]
        outp[b][rows] = res.results[d]["out"]
    return outp


# version 18 (final)
# speedup vs baseline: 1.0556x; 1.0556x over previous
import numpy as np
import ml_dtypes
import concourse.bass as bass
import concourse.mybir as mybir
from concourse.bass_utils import run_bass_kernel_spmd

F32 = mybir.dt.float32
BF16 = mybir.dt.bfloat16
I32 = mybir.dt.int32
Alu = mybir.AluOpType
Act = mybir.ActivationFunctionType
AX = mybir.AxisListType.X
bfdt = ml_dtypes.bfloat16

T, C, HP, DH, KVR = 4096, 1024, 8, 64, 128
R = 2048
NTILE = 16
NSLOT = 8
LN2C = 0.6931471805599453
EPS = 1e-5
OFFA = 10
OFFW = 14
NLAD = 16

DMA_ENGS = ("sync", "squeue", "vqueue", "gq")


class Sched:

    def __init__(self):
        self.ops = []
        self.ctr = {}

    def val(self, sem):
        return self.ctr.get(sem, 0)

    def em(self, eng, fn, waits=()):
        sem = "s_" + eng
        amt = 16 if eng in DMA_ENGS else 1
        self.ctr[sem] = self.ctr.get(sem, 0) + amt
        self.ops.append((eng, fn, list(waits), (sem, amt)))
        return self.ctr[sem]

    def emit(self, eng_names, eng, sems):
        for e, fn, waits, inc in self.ops:
            if e not in eng_names:
                continue
            for s, v in waits:
                if v > 0:
                    eng.wait_ge(sems[s], v)
            inst = fn(eng)
            assert inst is not None, f"op on {e} returned no instruction"
            inst.then_inc(sems[inc[0]], inc[1])


def _emit_round(S, em, V, S0, S1, S2, S3, I0, yq, scp):
    em("vector", lambda e: e.tensor_tensor(S1[:], S0[:], S0[:], Alu.mult))
    em("vector", lambda e: e.tensor_scalar(S2[:], S1[:], 1.0, -2.0, Alu.is_ge, Alu.mult))
    em("vector", lambda e: e.tensor_scalar(S3[:], S1[:], 4.0, -1.0, Alu.is_ge, Alu.mult))
    em("vector", lambda e: e.scalar_tensor_tensor(S1[:], S2[:], 4.0, S3[:], Alu.add, Alu.add))
    em("vector", lambda e: e.tensor_scalar(S3[:], S3[:], -0.5, 0.25, Alu.mult, Alu.add))
    em("vector", lambda e: e.scalar_tensor_tensor(S3[:], S2[:], -0.125, S3[:], Alu.mult, Alu.add))
    em("vector", lambda e: e.tensor_tensor(S2[:].rearrange("p (b c) -> p b c", c=32),
                                           S3[:].rearrange("p (b c) -> p b c", c=32),
                                           scp[:].unsqueeze(2).broadcast_to([128, 32, 32]), Alu.mult))
    em("vector", lambda e: e.tensor_tensor(S1[:], S0[:], S1[:], Alu.mult))
    em("vector", lambda e: e.tensor_scalar(S1[:], S1[:], 1.0, 0.5, Alu.mult, Alu.add))
    em("vector", lambda e: e.tensor_copy(I0[:], S1[:]))
    em("vector", lambda e: e.tensor_tensor(S0[:], I0[:], S1[:], Alu.is_gt))
    em("vector", lambda e: e.tensor_tensor(S0[:], I0[:], S0[:], Alu.subtract))
    em("vector", lambda e: e.tensor_tensor(yq[:], S0[:], S2[:], Alu.mult))


def build_program():
    import os
    KPHASE = int(os.environ.get("KPHASE", "9"))
    KNOBIAS = int(os.environ.get("KNOBIAS", "0"))
    KNTILE = int(os.environ.get("KNTILE", str(NTILE)))
    KSEC = int(os.environ.get("KSEC", "9"))
    KDBG = int(os.environ.get("KDBG", "0"))
    KATT = int(os.environ.get("KATT", "9"))
    nc = bass.Bass("TRN2", target_bir_lowering=False)

    x0 = nc.dram_tensor("x0", [R, C], F32, kind="ExternalInput").ap()
    fw = nc.dram_tensor("fw", [C, C], F32, kind="ExternalInput").ap()
    fb = nc.dram_tensor("fb", [1, C], F32, kind="ExternalInput").ap()
    fdw = nc.dram_tensor("fdw", [C, 32], F32, kind="ExternalInput").ap()
    fda = nc.dram_tensor("fda", [1, 1], F32, kind="ExternalInput").ap()
    wkd = nc.dram_tensor("wkd", [C, KVR], F32, kind="ExternalInput").ap()
    wku = nc.dram_tensor("wku", [KVR, 2048], F32, kind="ExternalInput").ap()
    wqa = nc.dram_tensor("wqa", [C, C], F32, kind="ExternalInput").ap()
    wo = nc.dram_tensor("wo", [C, C], F32, kind="ExternalInput").ap()
    thr = nc.dram_tensor("thr", [1, 2 * NLAD], F32, kind="ExternalInput").ap()
    maskh = nc.dram_tensor("maskh", [128, 4 * 256], BF16, kind="ExternalInput").ap()
    idt = nc.dram_tensor("idt", [128, 128], BF16, kind="ExternalInput").ap()
    out = nc.dram_tensor("out", [R, C], F32, kind="ExternalOutput").ap()
    dbgo = nc.dram_tensor("dbgo", [128, 4096], BF16, kind="ExternalOutput").ap()
    dbgf = nc.dram_tensor("dbgf", [128, 2048], F32, kind="ExternalOutput").ap()
    x1sp = nc.dram_tensor("x1sp", [R, C], F32)
    latmine = nc.dram_tensor("latmine", [128, R], BF16)
    latfull = nc.dram_tensor("latfull", [256, R], BF16)
    dab = nc.dram_tensor("dab", [1, 8], F32)
    drow = nc.dram_tensor("drow", [1, 512], F32)
    rrow = nc.dram_tensor("rrow", [1, 512], BF16)

    S = Sched()
    from contextlib import ExitStack
    es = ExitStack()
    sb = lambda name, shape, dt: es.enter_context(nc.sbuf_tensor(name, shape, dt))
    ps = lambda name, shape, dt: es.enter_context(nc.psum_tensor(name, shape, dt))

    xt = sb("xt", [128, 2, C], F32)
    x2t = sb("x2t", [128, 2, C], F32)
    outt = sb("outt", [128, 2, C], F32)
    qT = sb("qT", [128, HP, R], BF16)
    latT = sb("latT", [128, T], BF16)
    wqTf = sb("wqTf", [128, 8, C], BF16)
    wsh = sb("wsh", [128, 8, C], BF16)
    wkdb = sb("wkdb", [128, 8, KVR], BF16)
    wkub = sb("wkub", [128, 1024], BF16)
    wkuTb = sb("wkuTb", [128, HP, KVR], BF16)
    S0 = sb("S0", [128, C], F32)
    S1 = sb("S1", [128, C], F32)
    S2 = sb("S2", [128, C], BF16)
    S3 = sb("S3", [128, C], BF16)
    I0 = sb("I0", [128, C], I32)
    yq = sb("yq", [128, C], BF16)
    xn2 = sb("xn2", [128, C], BF16)
    xqT = sb("xqT", [128, 8, 128], BF16)
    xnT = sb("xnT", [128, 8, 128], BF16)
    vb = sb("vb", [128, 32, 2, 65], BF16)
    qaT = sb("qaT", [128, 2, R], BF16)
    yTall = sb("yTall", [128, HP, R], BF16)
    pb = sb("pb", [128, 3, 2, 2, 256], BF16)
    maskb = sb("maskb", [128, 4, 256], BF16)
    idtb = sb("idtb", [128, 128], BF16)
    onesb = sb("onesb", [1, 128], BF16)
    brad = sb("brad", [1, C], BF16)
    thrb = sb("thrb", [128, 2, NLAD], F32)
    thr2 = sb("thr2", [128, NLAD], F32)
    dasml = sb("dasml", [1, 8], F32)
    dabc = sb("dabc", [128, 8], F32)
    sm = sb("sm", [128, 16], F32)
    amx = sb("amx", [128, 32], F32)
    amn = sb("amn", [128, 32], F32)
    cnt = sb("cnt", [128, 32], F32)
    scp = sb("scp", [128, 32], BF16)
    invp = sb("invp", [128, 32], F32)
    eet = sb("eet", [128, 32], F32)
    gf = sb("gf", [128, 32], F32)
    dwt = sb("dwt", [128, 32], F32)
    ind = sb("ind", [128, 32, NLAD], BF16)
    Dcol = sb("Dcol", [128, 4], F32)
    rdcol = sb("rdcol", [128, 4], F32)
    rdcolb = sb("rdcolb", [128, 4], BF16)
    rdrow = sb("rdrow", [1, 512], BF16)
    ytmp = sb("ytmp", [128, 2, 256], F32)

    pSC = ps("pSC", [128, 2, 2, 2, 256], F32)
    pY = ps("pY", [128, 2, 2, 256], F32)
    pM = ps("pM", [128, 2, 512], F32)

    em, V = S.em, S.val
    pMv = pM.rearrange("p a b -> p (a b)")
    pMb = pM.bitcast(BF16)
    pSCv = pSC.rearrange("p a b c q -> p (a b c q)")

    def ln_quant(src, srcwaits, thr_idx, ksec=9):
        em("vector", lambda e: e.tensor_scalar(S0[:], src, 1.0 / C, 0.0, Alu.mult, Alu.add,
                                               accum_out=sm[:, 0:1]), srcwaits)
        em("vector", lambda e: e.scalar_tensor_tensor(S1[:], src, 1.0 / C, src, Alu.mult, Alu.mult,
                                                      accum_out=sm[:, 1:2]))
        em("vector", lambda e: e.scalar_tensor_tensor(sm[:, 3:4], sm[:, 0:1], -1.0, sm[:, 0:1],
                                                      Alu.mult, Alu.mult))
        em("vector", lambda e: e.tensor_scalar(S0[:], src, sm[:, 0:1], None, Alu.subtract))
        em("vector", lambda e: e.scalar_tensor_tensor(sm[:, 4:5], sm[:, 3:4], EPS, sm[:, 1:2],
                                                      Alu.add, Alu.add))
        DV0 = V("s_vector")
        em("scalar", lambda e: e.activation(sm[:, 5:6], sm[:, 4:5], Act.Ln), [("s_vector", DV0)])
        em("scalar", lambda e: e.activation(sm[:, 6:7], sm[:, 5:6], Act.Exp, scale=-0.5))
        em("scalar", lambda e: e.activation(sm[:, 7:8], sm[:, 5:6], Act.Exp, scale=0.5))
        ACT0 = V("s_scalar")
        if ksec == 0:
            return
        em("vector", lambda e: e.tensor_scalar(thr2[:], thrb[:, thr_idx, :], sm[:, 7:8], None,
                                               Alu.mult), [("s_scalar", ACT0)])
        em("vector", lambda e: e.reduce_max(amx[:], S0[:].rearrange("p (b c) -> p b c", c=32),
                                            axis=AX, apply_absolute_value=True))
        em("vector", lambda e: e.tensor_tensor(
            ind[:], amx[:].unsqueeze(2).broadcast_to([128, 32, NLAD]),
            thr2[:].unsqueeze(1).broadcast_to([128, 32, NLAD]), Alu.is_ge))
        em("vector", lambda e: e.reduce_sum(cnt[:], ind[:], axis=AX))
        em("vector", lambda e: e.drain())
        if ksec == 1:
            return
        em("vector", lambda e: e.tensor_scalar(amn[:], cnt[:], -LN2C, dabc[:, 2:3],
                                               Alu.mult, Alu.add))
        em("vector", lambda e: e.tensor_scalar(eet[:], cnt[:], LN2C, -(OFFA + 3) * LN2C,
                                               Alu.mult, Alu.add))
        DV1 = V("s_vector")
        em("scalar", lambda e: e.activation(invp[:], amn[:], Act.Exp), [("s_vector", DV1)])
        em("scalar", lambda e: e.activation(scp[:], eet[:], Act.Exp))
        ACT1 = V("s_scalar")
        em("vector", lambda e: e.tensor_scalar(gf[:], invp[:], sm[:, 6:7], 0.5, Alu.mult, Alu.mult),
           [("s_scalar", ACT1)])
        em("vector", lambda e: e.drain())
        em("vector", lambda e: e.tensor_tensor(S0[:].rearrange("p (b c) -> p b c", c=32),
                                               S0[:].rearrange("p (b c) -> p b c", c=32),
                                               gf[:].unsqueeze(2).broadcast_to([128, 32, 32]),
                                               Alu.mult))
        em("vector", lambda e: e.tensor_scalar(S0[:], S0[:], 3.0, -3.0, Alu.min, Alu.max))
        if ksec == 2:
            return
        _emit_round(S, em, V, S0, S1, S2, S3, I0, yq, scp)

    def transpose8(srcbuf, dst, wait_v):
        for g in range(2):
            for dt_ in range(4):
                em("tensor", lambda e, g=g, dt_=dt_: e.matmul(
                    pMb[:, g, 128 * dt_:128 * (dt_ + 1)],
                    srcbuf[:, 128 * (4 * g + dt_):128 * (4 * g + dt_ + 1)], idtb[:],
                    is_transpose=True, start=True, stop=True, skip_group_check=True),
                   [("s_vector", wait_v)] if (g == 0 and dt_ == 0) else [])
            em("vector", lambda e, g=g: e.tensor_copy(
                dst[:, 4 * g:4 * (g + 1), :].rearrange("p a b -> p (a b)"),
                pMb[:, g, 0:512]), [("s_tensor", V("s_tensor"))])

    def ld(dst, src, waits=()):
        return em("sync", lambda e, d=dst, s=src: e.dma_start(out=d, in_=s), waits)

    ld(thrb[:, 0, :], thr[0:1, 0:NLAD].broadcast_to([128, NLAD]))
    ld(thrb[:, 1, :], thr[0:1, NLAD:2 * NLAD].broadcast_to([128, NLAD]))
    ld(maskb[:].rearrange("p a b -> p (a b)"), maskh)
    ld(idtb[:], idt)
    ld(dasml[0:1, 0:1], fda)
    ld(xt[0:1, 0, :], fb)
    LD0 = V("s_sync")
    em("vector", lambda e: e.memset(onesb[:], 1.0), [("s_sync", LD0)])
    em("vector", lambda e: e.memset(vb[:, :, :, 64:65], 1.0))

    def dv(fn, waits=()):
        em("vector", fn, waits)
        em("vector", lambda e: e.drain())
    dv(lambda e: e.tensor_scalar(dasml[0:1, 1:2], dasml[0:1, 0:1], 8.0, None, Alu.mult))
    em("scalar", lambda e: e.activation(dasml[0:1, 2:3], dasml[0:1, 1:2], Act.Exp),
       [("s_vector", V("s_vector"))])
    dv(lambda e: e.tensor_scalar(dasml[0:1, 2:3], dasml[0:1, 2:3], 1.0, 1.0,
                                 Alu.mult, Alu.add), [("s_scalar", V("s_scalar"))])
    dv(lambda e: e.reciprocal(dasml[0:1, 3:4], dasml[0:1, 2:3]))
    dv(lambda e: e.tensor_scalar(dasml[0:1, 0:1], dasml[0:1, 3:4], -1.0, 0.5,
                                 Alu.mult, Alu.add))
    dv(lambda e: e.tensor_scalar(dasml[0:1, 4:5], dasml[0:1, 0:1], LN2C, None, Alu.mult))
    em("scalar", lambda e: e.activation(dasml[0:1, 1:2], dasml[0:1, 4:5], Act.Exp),
       [("s_vector", V("s_vector"))])
    dv(lambda e: e.tensor_scalar(dasml[0:1, 5:6], dasml[0:1, 4:5], -1.0, None, Alu.mult))
    em("scalar", lambda e: e.activation(dasml[0:1, 3:4], dasml[0:1, 5:6], Act.Exp),
       [("s_vector", V("s_vector"))])
    dv(lambda e: e.tensor_scalar(dasml[0:1, 2:3], dasml[0:1, 4:5], -1.0,
                                 (OFFA + 3) * LN2C, Alu.mult, Alu.add))
    DA_ACT = V("s_scalar")
    em("sync", lambda e: e.dma_start(out=dab.ap(), in_=dasml[0:1, :]),
       [("s_scalar", DA_ACT), ("s_vector", V("s_vector"))])
    em("sync", lambda e: e.dma_start(out=dabc[:], in_=dab.ap().broadcast_to([128, 8])),
       [("s_sync", V("s_sync"))])
    DA_BC = V("s_sync")
    em("vector", lambda e: e.tensor_scalar(brad[0:1, :], xt[0:1, 0, :], dabc[0:1, 3:4], 0.5,
                                           Alu.mult, Alu.mult), [("s_sync", DA_BC)])

    w0 = V("s_vector")
    ld(xt[:, 1, :], wku[:, 0:1024], [("s_vector", w0)])
    em("vector", lambda e: e.tensor_copy(wsh[:, 0, :], xt[:, 1, :]), [("s_sync", V("s_sync"))])
    WKU_DONE = V("s_vector")
    for hp in range(HP):
        em("tensor", lambda e, hp=hp: e.matmul(
            pMb[:, hp % 2, 0:128], wsh[:, 0, 128 * hp:128 * (hp + 1)], idtb[:],
            is_transpose=True, start=True, stop=True, skip_group_check=True),
           [("s_vector", WKU_DONE)] if hp == 0 else
           ([("s_vector", V("s_vector"))] if hp >= 2 else []))
        em("vector", lambda e, hp=hp: e.tensor_copy(wkuTb[:, hp, :], pMb[:, hp % 2, 0:128]),
           [("s_tensor", V("s_tensor"))])
    WT_DONE = V("s_vector")
    WT_T = V("s_tensor")
    w0 = V("s_vector")
    ld(xt[:, 1, :], wku[:, 1024:2048], [("s_vector", w0)])
    em("vector", lambda e: e.tensor_copy(wkub[:, 0:1024], xt[:, 1, :]), [("s_sync", V("s_sync"))])
    for k in range(8):
        w0 = V("s_vector")
        em("squeue", lambda e, k=k: e.dma_start(out=xt[:, 0, :], in_=wqa[128 * k:128 * (k + 1), :]),
           [("s_vector", w0)])
        em("vector", lambda e, k=k: e.tensor_scalar(wsh[:, k, :], xt[:, 0, :], 0.125, None, Alu.mult),
           [("s_squeue", V("s_squeue"))] + ([("s_tensor", WT_T)] if k == 0 else []))
    for k in range(8):
        w0 = V("s_vector")
        ld(xt[:, 1, 0:KVR], wkd[128 * k:128 * (k + 1), :], [("s_vector", w0)])
        em("vector", lambda e, k=k: e.tensor_copy(wkdb[:, k, :], xt[:, 1, 0:KVR]),
           [("s_sync", V("s_sync"))])

    fw_read = {}
    for ot in range(8):
        stg = ot % 2
        if ot < 2:
            wv = V("s_vector")
        else:
            wv = fw_read[ot - 2]
        ld(xt[:, stg, :], fw[128 * ot:128 * (ot + 1), :], [("s_vector", wv)])
        WL = V("s_sync")
        ld(dwt[:], fdw[128 * ot:128 * (ot + 1), :], [("s_vector", V("s_vector"))])
        WL2 = V("s_sync")
        em("vector", lambda e: e.tensor_scalar(dwt[:], dwt[:], 8.0, None, Alu.mult), [("s_sync", WL2)])
        em("vector", lambda e: e.drain())
        em("scalar", lambda e: e.activation(dwt[:], dwt[:], Act.Exp), [("s_vector", V("s_vector"))])
        em("vector", lambda e: e.tensor_scalar(dwt[:], dwt[:], 1.0, 1.0, Alu.mult, Alu.add),
           [("s_scalar", V("s_scalar"))])
        em("vector", lambda e: e.drain())
        em("vector", lambda e: e.reciprocal(dwt[:], dwt[:]))
        em("vector", lambda e: e.drain())
        em("vector", lambda e, stg=stg: e.reduce_max(amx[:], xt[:, stg, :].rearrange("p (b c) -> p b c", c=32),
                                                     axis=AX, apply_absolute_value=True),
           [("s_sync", WL)])
        em("vector", lambda e: e.tensor_tensor(
            ind[:], amx[:].unsqueeze(2).broadcast_to([128, 32, NLAD]),
            thrb[:, 1, :].unsqueeze(1).broadcast_to([128, 32, NLAD]), Alu.is_ge))
        em("vector", lambda e: e.reduce_sum(cnt[:], ind[:], axis=AX))
        em("vector", lambda e: e.drain())
        em("vector", lambda e: e.tensor_scalar(eet[:], cnt[:], -LN2C, (OFFW + 2.5) * LN2C,
                                               Alu.mult, Alu.add))
        em("vector", lambda e: e.drain())
        em("vector", lambda e: e.scalar_tensor_tensor(eet[:], dwt[:], LN2C, eet[:], Alu.mult, Alu.add))
        DVW = V("s_vector")
        em("scalar", lambda e: e.activation(invp[:], eet[:], Act.Exp), [("s_vector", DVW)])
        em("vector", lambda e: e.tensor_scalar(amn[:], cnt[:], LN2C, -(OFFW + 1.5) * LN2C,
                                               Alu.mult, Alu.add))
        em("vector", lambda e: e.drain())
        em("vector", lambda e: e.scalar_tensor_tensor(amn[:], dwt[:], -LN2C, amn[:], Alu.mult, Alu.add))
        DVW2 = V("s_vector")
        em("scalar", lambda e: e.activation(scp[:], amn[:], Act.Exp), [("s_vector", DVW2)])
        ACTW = V("s_scalar")
        em("vector", lambda e: e.tensor_scalar(gf[:], invp[:], 0.5, None, Alu.mult),
           [("s_scalar", ACTW)])
        em("vector", lambda e: e.drain())
        em("vector", lambda e, stg=stg: e.tensor_tensor(S0[:].rearrange("p (b c) -> p b c", c=32),
                                                        xt[:, stg, :].rearrange("p (b c) -> p b c", c=32),
                                                        gf[:].unsqueeze(2).broadcast_to([128, 32, 32]),
                                                        Alu.mult))
        fw_read[ot] = V("s_vector")
        em("vector", lambda e: e.tensor_scalar(S0[:], S0[:], 3.0, -3.0, Alu.min, Alu.max))
        _emit_round(S, em, V, S0, S1, S2, S3, I0, yq, scp)
        YQD = V("s_vector")
        for g in range(2):
            for dt_ in range(4):
                em("tensor", lambda e, g=g, dt_=dt_: e.matmul(
                    pMb[:, g, 128 * dt_:128 * (dt_ + 1)],
                    yq[:, 128 * (4 * g + dt_):128 * (4 * g + dt_ + 1)], idtb[:],
                    is_transpose=True, start=True, stop=True, skip_group_check=True),
                   [("s_vector", YQD)] if (g == 0 and dt_ == 0) else [])
            em("vector", lambda e, g=g, ot=ot: e.tensor_copy(
                wqTf[:, 4 * g:4 * (g + 1), 128 * ot:128 * (ot + 1)],
                pMb[:, g, 0:512].rearrange("p (a b) -> p a b", a=4)),
               [("s_tensor", V("s_tensor"))])
    P0_V = V("s_vector")
    P0_G = V("s_gpsimd")
    if KPHASE < 1:
        return _finish(S, nc, es, V)

    tile_v = {}
    tile_g = {}
    x1sp_dma = {}
    for t in range(KNTILE):
        p = t % 2
        waits = []
        if t < 2:
            waits += [("s_vector", P0_V), ("s_gpsimd", P0_G)]
        else:
            waits += [("s_vector", tile_v[t - 2]), ("s_gpsimd", tile_g[t - 2])]
        ld(xt[:, p, :], x0[128 * t:128 * (t + 1), :], waits)
        XL = V("s_sync")
        if KSEC < 0:
            tile_v[t] = V("s_vector")
            continue
        ln_quant(xt[:, p, :], [("s_sync", XL)], 0, KSEC)
        tile_g[t] = V("s_gpsimd")
        YQD = V("s_vector")
        if KDBG == 1 and t == 0:
            em("vector", lambda e: e.tensor_copy(S1[:], yq[:]))
            em("vector", lambda e: e.tensor_copy(amn[:], scp[:]))
            em("sync", lambda e: e.dma_start(out=out[0:128, :], in_=S1[:]),
               [("s_vector", V("s_vector"))])
            em("sync", lambda e: e.dma_start(out=out[128:256, 0:16], in_=sm[:]))
            em("sync", lambda e: e.dma_start(out=out[128:256, 32:64], in_=amx[:]))
            em("sync", lambda e: e.dma_start(out=out[128:256, 64:96], in_=cnt[:]))
            em("sync", lambda e: e.dma_start(out=out[128:256, 96:128], in_=invp[:]))
            em("sync", lambda e: e.dma_start(out=out[128:256, 128:160], in_=amn[:]))
            em("sync", lambda e: e.dma_start(out=out[256:384, :], in_=S0[:]))
            DBGD = V("s_sync")
            em("vector", lambda e: e.tensor_copy(S1[:], S2[:]), [("s_sync", DBGD)])
            em("sync", lambda e: e.dma_start(out=out[384:512, :], in_=S1[:]),
               [("s_vector", V("s_vector"))])
            em("vector", lambda e: e.engine_nop(), [("s_sync", V("s_sync"))])
        if KSEC < 1:
            tile_v[t] = V("s_vector")
            continue
        transpose8(yq, xqT, YQD)
        TQD = V("s_vector")
        if KSEC < 2:
            tile_v[t] = V("s_vector")
            continue
        for nh in range(2):
            for k in range(8):
                em("tensor", lambda e, nh=nh, k=k: e.matmul(
                    pSCv[:, 512 * nh:512 * (nh + 1)], xqT[:, k, :], wqTf[:, k, 512 * nh:512 * (nh + 1)],
                    start=(k == 0), stop=(k == 7), skip_group_check=True),
                   [("s_vector", TQD)] if (nh == 0 and k == 0) else [])
            if not KNOBIAS:
                em("tensor", lambda e, nh=nh: e.matmul(
                    pSCv[:, 512 * nh:512 * (nh + 1)], onesb[0:1, :], brad[0:1, 512 * nh:512 * (nh + 1)],
                    start=False, stop=True, skip_group_check=True))
        MMF = V("s_tensor")
        em("vector", lambda e, p=p: e.scalar_tensor_tensor(
            x2t[:, p, :], pSCv[:, 0:1024], dabc[:, 1:2], xt[:, p, :], Alu.mult, Alu.add),
           [("s_tensor", MMF)] + ([("s_gq", x1sp_dma[t - 2])] if t >= 2 else []))
        tile_v[t] = V("s_vector")
        em("gq", lambda e, t=t, p=p: e.dma_start(out=x1sp.ap()[128 * t:128 * (t + 1), :],
                                                 in_=x2t[:, p, :]),
           [("s_vector", tile_v[t])])
        x1sp_dma[t] = V("s_gq")
        x1t = x2t[:, p, :]
        if KSEC < 3:
            continue
        em("vector", lambda e, x1t=x1t: e.tensor_scalar(S0[:], x1t, 1.0 / C, 0.0, Alu.mult, Alu.add,
                                                        accum_out=sm[:, 0:1]))
        em("vector", lambda e, x1t=x1t: e.scalar_tensor_tensor(S1[:], x1t, 1.0 / C, x1t,
                                                               Alu.mult, Alu.mult, accum_out=sm[:, 1:2]))
        em("vector", lambda e: e.scalar_tensor_tensor(sm[:, 3:4], sm[:, 0:1], -1.0, sm[:, 0:1],
                                                      Alu.mult, Alu.mult))
        em("vector", lambda e: e.drain())
        em("vector", lambda e: e.scalar_tensor_tensor(sm[:, 4:5], sm[:, 3:4], EPS, sm[:, 1:2],
                                                      Alu.add, Alu.add))
        DLN = V("s_vector")
        em("scalar", lambda e: e.activation(sm[:, 5:6], sm[:, 4:5], Act.Ln), [("s_vector", DLN)])
        em("scalar", lambda e: e.activation(sm[:, 6:7], sm[:, 5:6], Act.Exp, scale=-0.5))
        em("vector", lambda e, x1t=x1t: e.tensor_scalar(xn2[:], x1t, sm[:, 0:1], sm[:, 6:7],
                                                        Alu.subtract, Alu.mult),
           [("s_scalar", V("s_scalar"))])
        XN2D = V("s_vector")
        if KSEC < 4:
            continue
        transpose8(xn2, xnT, XN2D)
        TND = V("s_vector")
        if KSEC < 5:
            continue
        for k in range(8):
            em("tensor", lambda e, k=k: e.matmul(
                pM[:, 0, 0:128], wkdb[:, k, :], xnT[:, k, :], start=(k == 0), stop=(k == 7)),
               [("s_vector", TND)] if k == 0 else [])
        em("vector", lambda e, t=t: e.tensor_copy(latT[:, 128 * t:128 * (t + 1)], pM[:, 0, 0:128]),
           [("s_tensor", V("s_tensor"))])
        em("squeue", lambda e, t=t: e.dma_start(out=latmine.ap()[:, 128 * t:128 * (t + 1)],
                                                in_=latT[:, 128 * t:128 * (t + 1)]),
           [("s_vector", V("s_vector"))])
        if KDBG == 5 and t == 0:
            em("vector", lambda e: e.tensor_copy(S1[:], xn2[:]))
            em("sync", lambda e: e.dma_start(out=out[0:128, :], in_=S1[:]),
               [("s_vector", V("s_vector"))])
            em("vector", lambda e: e.tensor_copy(S1[:, 0:128], latT[:, 0:128]),
               [("s_sync", V("s_sync"))])
            em("sync", lambda e: e.dma_start(out=out[128:256, 0:128], in_=S1[:, 0:128]),
               [("s_vector", V("s_vector"))])
            em("vector", lambda e: e.engine_nop(), [("s_sync", V("s_sync"))])
        if KSEC < 6:
            continue
        for hp in range(HP):
            for k in range(8):
                em("tensor", lambda e, hp=hp, k=k: e.matmul(
                    pM[:, 1, 0:128], wsh[:, k, 128 * hp:128 * (hp + 1)], xnT[:, k, :],
                    start=(k == 0), stop=(k == 7)),
                   [("s_vector", V("s_vector"))] if k == 0 else [])
            em("vector", lambda e, hp=hp, t=t: e.tensor_copy(qT[:, hp, 128 * t:128 * (t + 1)],
                                                             pM[:, 1, 0:128]),
               [("s_tensor", V("s_tensor"))])
    P1_SYNC = V("s_sync")
    P1_SQ = V("s_squeue")
    P1_GQ = V("s_gq")
    P1_V = V("s_vector")
    if KPHASE < 2:
        return _finish(S, nc, es, V)

    em("gcc", lambda e: e.collective_compute(
        "AllGather", Alu.bypass, replica_groups=[[0, 1], [2, 3], [4, 5], [6, 7]],
        ins=[latmine.ap().opt()], outs=[latfull.ap().opt()]),
       [("s_sync", P1_SYNC), ("s_squeue", P1_SQ)])
    CC = V("s_gcc")
    for hh in range(2):
        for j in range(NSLOT):
            em("squeue", lambda e, hh=hh, j=j: e.dma_start(
                out=latT[:, 512 * j + 256 * hh:512 * j + 256 * hh + 256],
                in_=latfull.ap()[128 * hh:128 * (hh + 1), 256 * j:256 * (j + 1)]),
               [("s_gcc", CC)] if (hh == 0 and j == 0) else [])
    LATR = V("s_squeue")
    if KDBG == 6:
        for hh in range(2):
            em("vector", lambda e, hh=hh: e.tensor_copy(S1[:], latT[:, 0:1024]) if False else
               e.engine_nop(), [])
        S.ops.pop(); S.ctr["s_vector"] -= 1
        S.ops.pop(); S.ctr["s_vector"] -= 1
        for hh in range(2):
            for cc2 in range(2):
                em("squeue", lambda e, hh=hh, cc2=cc2: e.dma_start(
                    out=yq[:], in_=latfull.ap()[128 * hh:128 * (hh + 1), 1024 * cc2:1024 * (cc2 + 1)]),
                   [("s_gcc", CC)] + ([("s_vector", V("s_vector"))] if (hh + cc2) else []))
                em("vector", lambda e: e.tensor_copy(S1[:], yq[:]), [("s_squeue", V("s_squeue"))])
                em("sync", lambda e, hh=hh, cc2=cc2: e.dma_start(
                    out=out[128 * (2 * hh + cc2):128 * (2 * hh + cc2 + 1), :], in_=S1[:]),
                   [("s_vector", V("s_vector"))])
        em("vector", lambda e: e.engine_nop(), [("s_sync", V("s_sync"))])
    if KDBG == 10:
        em("vector", lambda e: e.tensor_copy(S1[:, 0:520],
                                             vb[:, 0:4, :, :].rearrange("p a b c -> p (a b c)")),
           [("s_squeue", LATR)])
        em("sync", lambda e: e.dma_start(out=out[0:128, 0:520], in_=S1[:, 0:520]),
           [("s_vector", V("s_vector"))])
        em("vector", lambda e: e.tensor_copy(S1[:], qaT[:, 0, 0:1024]), [("s_sync", V("s_sync"))])
        em("sync", lambda e: e.dma_start(out=out[128:256, :], in_=S1[:]),
           [("s_vector", V("s_vector"))])
        em("vector", lambda e: e.engine_nop(), [("s_sync", V("s_sync"))])
    if KDBG == 8:
        em("vector", lambda e: e.tensor_copy(S1[:], qT[:, 0, 0:1024]), [("s_squeue", LATR)])
        em("sync", lambda e: e.dma_start(out=out[0:128, :], in_=S1[:]),
           [("s_vector", V("s_vector"))])
        em("vector", lambda e: e.tensor_copy(S1[:, 0:128], wkuTb[:, 0, :]), [("s_sync", V("s_sync"))])
        em("vector", lambda e: e.tensor_copy(S1[:, 128:256], wkub[:, 0:128]))
        em("vector", lambda e: e.tensor_copy(S1[:, 256:512], latT[:, 0:256]))
        em("sync", lambda e: e.dma_start(out=out[128:256, 0:512], in_=S1[:, 0:512]),
           [("s_vector", V("s_vector"))])
        em("vector", lambda e: e.engine_nop(), [("s_sync", V("s_sync"))])
    if KDBG == 3:
        for c4 in range(4):
            em("vector", lambda e, c4=c4: e.tensor_copy(S1[:], latT[:, 1024 * c4:1024 * (c4 + 1)]),
               [("s_squeue", LATR)] + ([("s_sync", V("s_sync"))] if c4 > 0 else []))
            em("sync", lambda e, c4=c4: e.dma_start(out=out[128 * c4:128 * (c4 + 1), :], in_=S1[:]),
               [("s_vector", V("s_vector"))])
        em("vector", lambda e: e.engine_nop(), [("s_sync", V("s_sync"))])
    if KPHASE < 3:
        return _finish(S, nc, es, V)

    epi_v = {0: 0, 1: 0}
    epi_gq = {0: 0, 1: 0}
    for hp in range(HP):
        wv = V("s_vector")
        wow = [("s_vector", max(wv, P1_V))]
        if hp == 0 and tile_g:
            wow.append(("s_gpsimd", tile_g[max(tile_g)]))
        ld(xt[:, 1, :], wo[128 * hp:128 * (hp + 1), :], wow)
        WO_LD = V("s_sync")
        for h01 in range(2 if KATT >= 1 else 0):
            qw = [("s_squeue", LATR), ("s_scalar", V("s_scalar")), ("s_vector", V("s_vector"))] \
                if (hp == 0 and h01 == 0) else [("s_scalar", V("s_scalar")),
                                                ("s_vector", V("s_vector"))]
            for cch in range(4):
                em("tensor", lambda e, hp=hp, h01=h01, cch=cch: e.matmul(
                    pSCv[:, 512 * cch:512 * (cch + 1)],
                    wkuTb[64 * h01:64 * (h01 + 1), hp, :],
                    qT[64 * h01:64 * (h01 + 1), hp, 512 * cch:512 * (cch + 1)],
                    start=True, stop=True, skip_group_check=True),
                   qw if cch == 0 else [])
            MMQ = V("s_tensor")
            em("vector", lambda e, h01=h01: e.tensor_copy(qaT[:, h01, :], pSCv[:, 0:2048]),
               [("s_tensor", MMQ)])
        QAD = V("s_vector")
        for g in range(8 if KATT >= 2 else 0):
            vw = [("s_vector", V("s_vector"))]
            for kt4 in range(4):
                kt = 4 * g + kt4
                for h01 in range(2):
                    em("tensor", lambda e, kt=kt, kt4=kt4, h01=h01, hp=hp: e.matmul(
                        pMv[:, 64 * (2 * kt4 + h01):64 * (2 * kt4 + h01) + 64],
                        latT[:, 128 * kt:128 * (kt + 1)],
                        wkub[:, 128 * hp + 64 * h01:128 * hp + 64 * (h01 + 1)],
                        start=True, stop=True, skip_group_check=True),
                       vw if (kt4 == 0 and h01 == 0) else [])
            MMV = V("s_tensor")
            if KATT >= 3:
                em("vector", lambda e, g=g: e.tensor_copy(
                    vb[:, 4 * g:4 * (g + 1), :, 0:64],
                    pMv[:, 0:512].rearrange("p (a b c) -> p a b c", a=4, b=2)),
                   [("s_tensor", MMV)])
        VD = V("s_vector")
        if KDBG == 12 and hp == 0:
            em("sync", lambda e: e.dma_start(
                out=dbgo[:, 0:1040], in_=vb[:, 0:8, :, :].rearrange("p a b c -> p (a b c)")),
               [("s_vector", VD), ("s_tensor", V("s_tensor"))])
            em("sync", lambda e: e.dma_start(out=dbgo[:, 1024 + 0:1024 + 2048], in_=qaT[:, 0, :]))
            em("vector", lambda e: e.engine_nop(), [("s_sync", V("s_sync"))])
        if KDBG == 11 and hp == 0:
            em("vector", lambda e: e.tensor_copy(S1[:, 0:520],
                                                 vb[:, 0:4, :, :].rearrange("p a b c -> p (a b c)")),
               [("s_vector", VD)])
            em("sync", lambda e: e.dma_start(out=out[0:128, 0:520], in_=S1[:, 0:520]),
               [("s_vector", V("s_vector"))])
            em("vector", lambda e: e.tensor_copy(S1[:], qaT[:, 0, 0:1024]), [("s_sync", V("s_sync"))])
            em("sync", lambda e: e.dma_start(out=out[128:256, :], in_=S1[:]),
               [("s_vector", V("s_vector"))])
            em("vector", lambda e: e.engine_nop(), [("s_sync", V("s_sync"))])
        gb3 = 0
        for j in range(NSLOT if KATT >= 9 else 0):
            nkt = 4 * j + 4
            nb = nkt // 2
            sp = j % 2
            pending = []
            for kb in range(nb):
                buf = kb % 2
                buf3 = gb3 % 3
                gb3 += 1
                scw = [("s_vector", VD), ("s_scalar", max(0, V("s_scalar") - 1))]
                if kb == 0:
                    scw += [("s_vector", epi_v[sp]), ("s_gq", epi_gq[sp])]
                for kk in range(2):
                    kt = 2 * kb + kk
                    for h01 in range(2):
                        em("tensor", lambda e, kt=kt, kk=kk, h01=h01, buf=buf, j=j: e.matmul(
                            pSC[:, buf, h01, kk, :], latT[:, 128 * kt:128 * (kt + 1)],
                            qaT[:, h01, 256 * j:256 * (j + 1)], start=True, stop=True,
                            skip_group_check=True),
                           scw if (kk == 0 and h01 == 0) else [])
                MMS = V("s_tensor")
                em("scalar", lambda e, buf=buf, buf3=buf3: e.activation(
                    pb.rearrange("p a b c q -> p a (b c q)")[:, buf3, :],
                    pSCv[:, 1024 * buf:1024 * (buf + 1)], Act.Exp), [("s_tensor", MMS)])
                EXPD = V("s_scalar")

                def make_av(kb=kb, buf3=buf3, sp=sp, nkt=nkt, nb=nb, EXPD=EXPD):
                    avw = [("s_scalar", EXPD)]
                    if kb >= nb - 2:
                        r = kb - (nb - 2)
                        em("vector", lambda e, buf3=buf3, r=r: e.tensor_tensor(
                            pb[:, buf3, :, :, :], pb[:, buf3, :, :, :],
                            maskb[:, 2 * r:2 * r + 2, :].unsqueeze(1).broadcast_to([128, 2, 2, 256]),
                            Alu.mult), [("s_scalar", EXPD)])
                        avw = [("s_vector", V("s_vector"))]
                    for kk in range(2):
                        kt = 2 * kb + kk
                        for h01 in range(2):
                            em("tensor", lambda e, kt=kt, h01=h01, buf3=buf3, sp=sp, kk=kk, nkt=nkt: e.matmul(
                                pY[0:65, h01, sp, :], vb[:, kt, h01, :], pb[:, buf3, h01, kk, :],
                                start=(kt == 0), stop=(kt == nkt - 1), skip_group_check=True),
                               avw if (kk == 0 and h01 == 0) else [])

                pending.append(make_av)
                if len(pending) > 2:
                    pending.pop(0)()
            for f in pending:
                f()
            AVD = V("s_tensor")
            em("vector", lambda e, sp=sp: e.tensor_copy(ytmp[0:65, :, :], pY[0:65, :, sp, :]),
               [("s_tensor", AVD), ("s_gq", V("s_gq"))])
            YE = V("s_vector")
            if KDBG == 16 and hp == 0 and j == 0:
                em("sync", lambda e: e.dma_start(out=dbgf[:, 0:512],
                                                 in_=ytmp[:].rearrange("p a b -> p (a b)")),
                   [("s_vector", YE)])
                DSY = V("s_sync")
                em("vector", lambda e: e.engine_nop(), [("s_sync", DSY)])
                em("tensor", lambda e: e.nop(), [("s_sync", DSY)])
                em("scalar", lambda e: e.nop(), [("s_sync", DSY)])
            em("gq", lambda e: e.dma_start(out=drow.ap(), in_=ytmp[64:65, :, :]),
               [("s_vector", YE)])
            em("gq", lambda e: e.dma_start(
                out=Dcol[:], in_=drow.ap().rearrange("a (p c) -> (a p) c", p=128)),
               [("s_gq", V("s_gq"))])
            em("vector", lambda e: e.reciprocal(rdcol[:], Dcol[:]), [("s_gq", V("s_gq"))])
            em("vector", lambda e: e.drain())
            em("vector", lambda e: e.tensor_copy(rdcolb[:], rdcol[:]))
            em("gq", lambda e: e.dma_start(
                out=rrow.ap().rearrange("a (p c) -> (a p) c", p=128), in_=rdcolb[:]),
               [("s_vector", V("s_vector"))])
            em("gq", lambda e: e.dma_start(out=rdrow[:], in_=rrow.ap()), [("s_gq", V("s_gq"))])
            RD = V("s_gq")
            em("tensor", lambda e: e.matmul(pMv[:, 0:512], onesb[0:1, :], rdrow[0:1, :],
                                            start=True, stop=True, skip_group_check=True),
               [("s_gq", RD), ("s_vector", V("s_vector"))])
            BCD = V("s_tensor")
            em("gq", lambda e: e.dma_start(out=ytmp[64:128, 1, :], in_=ytmp[0:64, 1, :]),
               [("s_vector", YE)])
            YMV = V("s_gq")
            em("vector", lambda e, hp=hp, j=j: e.tensor_tensor(
                yTall[0:64, hp, 256 * j:256 * (j + 1)],
                ytmp[0:64, 0, :], pMv[0:64, 0:256], Alu.mult),
               [("s_tensor", BCD), ("s_vector", YE)])
            em("vector", lambda e, hp=hp, j=j: e.tensor_tensor(
                yTall[64:128, hp, 256 * j:256 * (j + 1)],
                ytmp[64:128, 1, :], pMv[64:128, 256:512], Alu.mult),
               [("s_gq", YMV)])
            epi_v[sp] = V("s_vector")
            epi_gq[sp] = V("s_gq")
            if KDBG == 13 and hp == 0 and j == 0:
                em("sync", lambda e: e.dma_start(out=dbgf[:, 0:512],
                                                 in_=ytmp[:].rearrange("p a b -> p (a b)")),
                   [("s_vector", V("s_vector")), ("s_gq", V("s_gq"))])
                em("sync", lambda e: e.dma_start(out=dbgf[:, 512:516], in_=rdcol[:]))
                em("sync", lambda e: e.dma_start(out=dbgf[:, 516:520], in_=Dcol[:]))
                em("sync", lambda e: e.dma_start(
                    out=dbgo[:, 0:512],
                    in_=pb[:, 0, :, :, :].rearrange("p a b q -> p (a b q)")[:, 0:512]))
                em("sync", lambda e: e.dma_start(out=dbgo[0:1, 512:1024], in_=rdrow[0:1, :]))
                em("vector", lambda e: e.engine_nop(), [("s_sync", V("s_sync"))])
            if KDBG == 9 and hp == 0 and j == 0:
                em("vector", lambda e: e.tensor_copy(S1[:], qaT[:, 0, 0:1024]))
                em("sync", lambda e: e.dma_start(out=out[256:384, :], in_=S1[:]),
                   [("s_vector", V("s_vector"))])
                em("vector", lambda e: e.tensor_copy(S1[:, 0:520],
                                                     vb[:, 0:4, :, :].rearrange("p a b c -> p (a b c)")),
                   [("s_sync", V("s_sync"))])
                em("sync", lambda e: e.dma_start(out=out[384:512, 0:520], in_=S1[:, 0:520]),
                   [("s_vector", V("s_vector"))])
                em("sync", lambda e: e.dma_start(out=out[0:128, 0:512],
                                                 in_=ytmp[:].rearrange("p a b -> p (a b)")))
                em("vector", lambda e: e.engine_nop(), [("s_sync", V("s_sync"))])
            if KDBG == 7 and hp == 0 and j == 0:
                EPIV = V("s_vector")
                em("sync", lambda e: e.dma_start(out=out[0:128, 0:512],
                                                 in_=ytmp[:].rearrange("p a b -> p (a b)")),
                   [("s_vector", EPIV)])
                em("sync", lambda e: e.dma_start(out=out[128:256, 0:4], in_=rdcol[:]))
                em("gq", lambda e: e.dma_start(out=out[256:384, 0:512],
                                               in_=qaT[:, :, 0:256]
                                               .rearrange("p a b -> p a b")),
                   [("s_vector", EPIV)])
                em("gq", lambda e: e.dma_start(out=out[384:512, 0:260],
                                               in_=vb[:, 0:2, :, :].rearrange("p a b c -> p (a b c)")),
                   [("s_vector", EPIV)])
                em("vector", lambda e: e.engine_nop(),
                   [("s_sync", V("s_sync")), ("s_gq", V("s_gq"))])
        em("vector", lambda e, hp=hp: e.tensor_copy(wsh[:, hp, :], xt[:, 1, :]),
           [("s_sync", WO_LD)])
    if KDBG == 17:
        em("sync", lambda e: e.dma_start(out=dbgo[:, 0:2048], in_=yTall[:, 0, :]),
           [("s_vector", V("s_vector"))])
        em("sync", lambda e: e.dma_start(out=dbgo[:, 2048:4096], in_=yTall[:, 7, :]))
        DSY = V("s_sync")
        em("vector", lambda e: e.engine_nop(), [("s_sync", DSY)])
        em("tensor", lambda e: e.nop(), [("s_sync", DSY)])
    if KDBG == 14:
        for hpx in range(HP):
            em("sync", lambda e, hpx=hpx: e.dma_start(out=dbgo[:, 256 * hpx:256 * (hpx + 1)],
                                                      in_=yTall[:, hpx, 0:256]),
               [("s_vector", V("s_vector"))] if hpx == 0 else [])
        em("vector", lambda e: e.engine_nop(), [("s_sync", V("s_sync"))])
    P3_V = V("s_vector")
    WO_DONE = P3_V
    if KPHASE < 4:
        return _finish(S, nc, es, V)

    if KDBG == 18:
        em("sync", lambda e: e.dma_start(out=dbgo[:, 0:1024], in_=wsh[:, 0, :]),
           [("s_vector", P3_V)])
        em("sync", lambda e: e.dma_start(out=dbgo[:, 1024:2048], in_=wsh[:, 7, :]))
        em("sync", lambda e: e.dma_start(out=dbgo[:, 2048:3072], in_=wsh[:, 1, :]))
        DSY = V("s_sync")
        em("vector", lambda e: e.engine_nop(), [("s_sync", DSY)])
        em("tensor", lambda e: e.nop(), [("s_sync", DSY)])
    x1ld = {}
    tile_v4 = {}
    for t in range(NTILE):
        p = t % 2
        xw = [("s_vector", P3_V)] if t < 2 else [("s_vector", tile_v4[t - 2])]
        if t == 0:
            xw.append(("s_gq", P1_GQ))
        em("sync", lambda e, t=t, p=p: e.dma_start(out=xt[:, p, :],
                                                   in_=x1sp.ap()[128 * t:128 * (t + 1), :]), xw)
        x1ld[t] = V("s_sync")
        zw = [("s_vector", P3_V if t == 0 else V("s_vector")), ("s_vector", WO_DONE)]
        for nh in range(2):
            for hp in range(HP):
                em("tensor", lambda e, t=t, nh=nh, hp=hp: e.matmul(
                    pM[:, nh, :], yTall[:, hp, 128 * t:128 * (t + 1)],
                    wsh[:, hp, 512 * nh:512 * (nh + 1)], start=(hp == 0), stop=(hp == 7)),
                   zw if (nh == 0 and hp == 0) else [])
        MMZ = V("s_tensor")
        em("vector", lambda e, t=t, p=p: e.scalar_tensor_tensor(
            x2t[:, p, :], pMv[:, 0:1024], 1.0, xt[:, p, :], Alu.mult, Alu.add),
           [("s_tensor", MMZ), ("s_sync", x1ld[t])])
        X2D = V("s_vector")
        if KDBG == 15 and t == 0:
            em("sync", lambda e: e.dma_start(out=dbgf[:, 0:1024], in_=x2t[:, 0, :]),
               [("s_vector", X2D)])
            em("vector", lambda e: e.engine_nop(), [("s_sync", V("s_sync"))])
        ln_quant(x2t[:, p, :], [("s_vector", X2D)], 0)
        YQD = V("s_vector")
        transpose8(yq, xqT, YQD)
        TQD = V("s_vector")
        for nh in range(2):
            for k in range(8):
                em("tensor", lambda e, nh=nh, k=k: e.matmul(
                    pSCv[:, 512 * nh:512 * (nh + 1)], xqT[:, k, :], wqTf[:, k, 512 * nh:512 * (nh + 1)],
                    start=(k == 0), stop=(k == 7), skip_group_check=True),
                   [("s_vector", TQD)] if (nh == 0 and k == 0) else [])
            em("tensor", lambda e, nh=nh: e.matmul(
                pSCv[:, 512 * nh:512 * (nh + 1)], onesb[0:1, :], brad[0:1, 512 * nh:512 * (nh + 1)],
                start=False, stop=True, skip_group_check=True))
        MMF = V("s_tensor")
        ow = [("s_tensor", MMF)]
        if t >= 2:
            ow.append(("s_vqueue", 16 * (t - 1)))
        em("vector", lambda e, p=p: e.scalar_tensor_tensor(
            outt[:, p, :], pSCv[:, 0:1024], dabc[:, 1:2], x2t[:, p, :], Alu.mult, Alu.add), ow)
        tile_v4[t] = V("s_vector")
        em("vqueue", lambda e, t=t, p=p: e.dma_start(out=out[128 * t:128 * (t + 1), :],
                                                     in_=outt[:, p, :]),
           [("s_vector", V("s_vector"))])
    return _finish(S, nc, es, V)


def _finish(S, nc, es, V):
    final_waits = [(s, V(s)) for s in ["s_vqueue", "s_vector", "s_scalar", "s_tensor",
                                       "s_gpsimd", "s_gq", "s_squeue"] if V(s) > 0]
    sem_names = ["s_sync", "s_squeue", "s_vqueue", "s_gq", "s_gcc", "s_tensor", "s_vector",
                 "s_scalar", "s_gpsimd"]
    with nc.Block() as block:
        sems = {s: es.enter_context(nc.semaphore(s)) for s in sem_names}

        @block.sync
        def _(sync):
            S.emit(("sync",), sync, sems)
            for s, v in final_waits:
                sync.wait_ge(sems[s], v)

        @block.vector
        def _(v):
            S.emit(("vector",), v, sems)

        @block.scalar
        def _(s):
            S.emit(("scalar", "squeue", "vqueue"), s, sems)

        @block.tensor
        def _(tn):
            S.emit(("tensor",), tn, sems)

        @block.gpsimd
        def _(g):
            S.emit(("gpsimd", "gq", "gcc"), g, sems)

    es.close()
    return nc


_CACHE = {}


def _get_nc():
    if "nc" not in _CACHE:
        _CACHE["nc"] = build_program()
    return _CACHE["nc"]


def _host_consts(h):
    key = f"c{h}"
    if key in _CACHE:
        return _CACHE[key]
    m = np.zeros((128, 4, 256), dtype=bfdt)
    kp = np.arange(128)[:, None]
    qf = np.arange(256)[None, :]
    for r in range(4):
        off = 128 * r - 256 * h
        m[:, r, :] = ((off + kp) <= qf).astype(bfdt)
    mask = np.ascontiguousarray(m.reshape(128, 4 * 256))
    thrv = np.zeros((1, 2 * NLAD), dtype=np.float32)
    thrv[0, :NLAD] = 2.0 ** np.arange(-OFFA, -OFFA + NLAD)
    thrv[0, NLAD:] = 2.0 ** np.arange(-OFFW, -OFFW + NLAD)
    idv = np.eye(128, dtype=bfdt)
    _CACHE[key] = (mask, thrv, idv)
    return _CACHE[key]


def kernel(L0_attn_x, L0_attn_w_kv_down, L0_attn_w_kv_up, L0_attn_w_q, L0_attn_w_out,
           L0_ffn_weight, L0_ffn_bias, L0_ffn_delta_w, L0_ffn_delta_act):
    x = np.ascontiguousarray(np.asarray(L0_attn_x, dtype=np.float32))
    wkdv = np.ascontiguousarray(np.asarray(L0_attn_w_kv_down, dtype=np.float32))
    wkuv = np.ascontiguousarray(np.asarray(L0_attn_w_kv_up, dtype=np.float32))
    wqv = np.ascontiguousarray(np.asarray(L0_attn_w_q, dtype=np.float32))
    wov = np.ascontiguousarray(np.asarray(L0_attn_w_out, dtype=np.float32))
    fwv = np.ascontiguousarray(np.asarray(L0_ffn_weight, dtype=np.float32))
    fbv = np.ascontiguousarray(np.asarray(L0_ffn_bias, dtype=np.float32)).reshape(1, C)
    fdwv = np.ascontiguousarray(np.asarray(L0_ffn_delta_w, dtype=np.float32))
    fdav = np.ascontiguousarray(np.asarray(L0_ffn_delta_act, dtype=np.float32)).reshape(1, 1)

    nc = _get_nc()
    in_maps = []
    rowsets = []
    for d in range(8):
        b, h = d // 2, d % 2
        mask, thrv, idv = _host_consts(h)
        rows = np.concatenate([np.arange(256 * (2 * j + h), 256 * (2 * j + h) + 256)
                               for j in range(NSLOT)])
        rowsets.append((b, rows))
        in_maps.append(dict(
            x0=np.ascontiguousarray(x[b][rows]),
            fw=fwv, fb=fbv, fdw=fdwv, fda=fdav,
            wkd=wkdv, wku=wkuv, wqa=wqv, wo=wov,
            thr=thrv, maskh=mask, idt=idv,
        ))
    res = run_bass_kernel_spmd(nc, in_maps, core_ids=list(range(8)))
    _CACHE["res"] = res
    outp = np.empty((4, T, C), dtype=np.float32)
    for d in range(8):
        b, rows = rowsets[d]
        outp[b][rows] = res.results[d]["out"]
    return outp
